# version 10
# baseline (speedup 1.0000x reference)
import contextlib
import numpy as np

import concourse.bass as bass
import concourse.mybir as mybir
from concourse.bass_utils import run_bass_kernel_spmd

F32 = mybir.dt.float32
BF16 = mybir.dt.bfloat16
AF = mybir.ActivationFunctionType
ALU = mybir.AluOpType
AX = mybir.AxisListType.X

B, T, DM = 4, 4096, 1024
NH, DH = 16, 64
DLAT = 256
DEPTH = 6
WIN = 256
DI = 2730
DI_PAD = 2816
KCH = DI_PAD // 128
T_CORE = 2944
START1 = T - T_CORE
NTILE = 512
KSPAN = 384
NCH = 8
MASK_NEG = -1.0e30
TILES = [NTILE] * (T_CORE // NTILE) + ([T_CORE % NTILE] if T_CORE % NTILE else [])
FTILE = 448
FTILES = [FTILE] * (T_CORE // FTILE) + ([T_CORE % FTILE] if T_CORE % FTILE else [])
XSTR = NTILE + 1
KSTR = WIN + NTILE

PIPELINE = ["mla"] + [f"{p}{l}" for l in range(DEPTH) for p in ("att", "ffn")] + ["fin"]
DEBUG_STREAMS = False
DEBUG_TAPS = ()


class Op:
    __slots__ = ("idx", "eng", "emit", "dma", "inc", "milestone", "waits", "tag",
                 "deps", "wait_only", "dma_sem")

    def __init__(self, idx, eng, emit, dma, tag):
        self.idx, self.eng, self.emit, self.dma, self.tag = idx, eng, emit, dma, tag
        self.inc = False
        self.milestone = None
        self.waits = []
        self.deps = []


class Em:
    ENGINES = ("sync", "tensor", "vector", "scalar", "gpsimd")
    NDMA_SEMS = 32

    def __init__(self):
        self.ops = []
        self.last_writer = {}
        self.readers = {}
        self.last_on = {e: None for e in self.ENGINES}
        self.last_real = {e: None for e in self.ENGINES}

    def op(self, eng, emit, reads=(), writes=(), extra_deps=(), tag="", wait_only=False):
        dma = (eng == "sync") and not wait_only
        i = len(self.ops)
        o = Op(i, eng, emit, dma, tag)
        o.wait_only = wait_only
        deps = set(x for x in extra_deps if x is not None)
        for r in reads:
            if r in self.last_writer:
                deps.add(self.last_writer[r])
        for w in writes:
            if w in self.last_writer:
                deps.add(self.last_writer[w])
            for rd in self.readers.get(w, ()):
                deps.add(rd)
        for r in reads:
            self.readers.setdefault(r, []).append(i)
        for w in writes:
            self.last_writer[w] = i
            self.readers[w] = []
        o.deps = sorted(d for d in deps if d != i)
        self.ops.append(o)
        self.last_on[eng] = i
        if not wait_only:
            self.last_real[eng] = i
        return i

    def barrier(self, tag="bar"):
        lasts = dict(self.last_real)
        for e in self.ENGINES:
            deps = [lasts[e2] for e2 in self.ENGINES if e2 != e and lasts[e2] is not None]
            self.op(e, lambda eng: None, extra_deps=deps, tag=f"{tag}:{e}", wait_only=True)

    def finalize_and_emit(self, block, sems):
        ops = self.ops
        for o in ops:
            nd = set()
            stack = list(o.deps)
            while stack:
                d = stack.pop()
                if ops[d].wait_only:
                    stack.extend(ops[d].deps)
                else:
                    nd.add(d)
            o.deps = sorted(nd)
        for o in ops:
            for d in o.deps:
                if ops[d].eng != o.eng:
                    ops[d].inc = True
        counts = {e: 0 for e in self.ENGINES}
        dma_counts = [0] * self.NDMA_SEMS
        dma_i = 0
        for o in ops:
            if o.dma:
                o.inc = True
                o.dma_sem = dma_i % self.NDMA_SEMS
                dma_i += 1
                dma_counts[o.dma_sem] += 16
                o.milestone = dma_counts[o.dma_sem]
            else:
                if o.inc:
                    counts[o.eng] += 1
                o.milestone = counts[o.eng]
        for o in ops:
            need = {}
            for d in o.deps:
                p = ops[d]
                if p.dma:
                    k = f"dma{p.dma_sem}"
                    need[k] = max(need.get(k, 0), p.milestone)
                elif p.eng != o.eng:
                    need[p.eng] = max(need.get(p.eng, 0), p.milestone)
            o.waits = sorted(need.items())
        allsrc = list(self.ENGINES) + [f"dma{i}" for i in range(self.NDMA_SEMS)]
        seen = {e: {e2: 0 for e2 in allsrc} for e in self.ENGINES}
        for o in ops:
            w2 = []
            for (src, val) in o.waits:
                if seen[o.eng][src] < val:
                    seen[o.eng][src] = val
                    w2.append((src, val))
            o.waits = w2
        per_eng = {e: [o for o in ops if o.eng == e] for e in self.ENGINES}

        def mkrun(eng_name):
            def run(eng):
                for o in per_eng[eng_name]:
                    for (src, val) in o.waits:
                        eng.wait_ge(sems[src], val)
                    ins = o.emit(eng)
                    if o.inc:
                        assert ins is not None, f"op {o.idx} [{o.tag}] inc w/o instruction"
                        if o.dma:
                            ins.then_inc(sems[f"dma{o.dma_sem}"], 16)
                        else:
                            ins.then_inc(sems[o.eng], 1)
            return run

        block.sync(mkrun("sync"))
        block.tensor(mkrun("tensor"))
        block.vector(mkrun("vector"))
        block.scalar(mkrun("scalar"))
        block.gpsimd(mkrun("gpsimd"))


def _rope_tables(t_abs):
    t64 = t_abs.astype(np.float64)
    j = np.arange(32, dtype=np.float64)
    inv = 10000.0 ** (-(2.0 * j) / 64.0)
    ang = np.outer(t64, inv)
    cos_, sin_ = np.cos(ang), np.sin(ang)
    bsc = (2.0 * j + 0.4 * 64.0) / (1.4 * 64.0)
    power = (t64 - (T // 2)) / (WIN // 2)
    scale = bsc[None, :] ** power[:, None]
    d = np.arange(128) % 64
    jj = d % 32
    sgn = np.where(d < 32, -1.0, 1.0)

    def FM(ct, st):
        return (np.ascontiguousarray(ct[:, jj].T.astype(np.float32)),
                np.ascontiguousarray((st[:, jj].T * sgn[:, None]).astype(np.float32)))

    C0, S0 = FM(cos_, sin_)
    Cq, Sq = FM(cos_ * scale, sin_ * scale)
    Ck, Sk = FM(cos_ / scale, sin_ / scale)
    return {"C0": C0, "S0": S0, "Cq": Cq, "Sq": Sq, "Ck": Ck, "Sk": Sk}


def _masks(first):
    r = np.arange(128)[:, None]
    jx = np.arange(KSPAN)[None, :]
    band = (jx >= r) & (jx <= r + WIN)
    m = np.zeros((3, 128, KSPAN), np.float32)
    for s in range(3):
        vv = band.copy()
        if first and s == 0:
            vv &= jx >= 256
        if first and s == 1:
            vv &= jx >= 128
        m[s][~vv] = MASK_NEG
    return np.ascontiguousarray(m)


def _cols(vec, nch):
    return np.ascontiguousarray(np.asarray(vec, np.float32).reshape(nch, 128).T)


def _host_weights(inputs):
    W = {}
    f = lambda a: np.ascontiguousarray(np.asarray(a, dtype=np.float32))
    W["Wq"] = f(inputs["L0_attn_Wq"])
    W["bq"] = _cols(inputs["L0_attn_bq"], 8)
    W["Wdown"] = f(inputs["L0_attn_Wdown"])
    W["bdown"] = np.ascontiguousarray(
        np.asarray(inputs["L0_attn_bdown"], np.float32).reshape(2, 128).T)
    kv = f(inputs["L0_attn_Wup"]).reshape(DLAT, NH, 2 * DH)
    W["Wup_k"] = np.ascontiguousarray(kv[:, :, :DH].reshape(DLAT, DM))
    W["Wup_v"] = np.ascontiguousarray(
        np.transpose(kv[:, :, DH:], (0, 2, 1)).reshape(DLAT, DM))
    bkv = f(inputs["L0_attn_bup"]).reshape(NH, 2 * DH)
    W["bup_k"] = _cols(bkv[:, :DH].reshape(DM), 8)
    W["bup_v_row"] = np.ascontiguousarray(
        np.transpose(bkv[:, DH:], (1, 0)).reshape(1, DM))
    W["Wout"] = f(inputs["L0_attn_Wout"])
    W["bout"] = _cols(inputs["L0_attn_bout"], 8)

    for l in range(DEPTH):
        W[f"Wqkv{l}"] = f(inputs["L0_ffn_Wqkv"][l])
        W[f"Wo{l}"] = f(inputs["L0_ffn_Wo"][l])
        W1 = f(inputs["L0_ffn_W1"][l])
        b1 = f(inputs["L0_ffn_b1"][l])
        pad = np.zeros((DM, DI_PAD - DI), np.float32)
        W[f"W1a{l}"] = np.ascontiguousarray(np.concatenate([W1[:, :DI], pad], 1))
        W[f"W1g{l}"] = np.ascontiguousarray(np.concatenate([W1[:, DI:], pad], 1))
        zp = np.zeros(DI_PAD - DI, np.float32)
        W[f"b1a{l}"] = _cols(np.concatenate([b1[:DI], zp]), KCH)
        W[f"b1g{l}"] = _cols(np.concatenate([b1[DI:], zp]), KCH)
        W[f"W2{l}"] = np.ascontiguousarray(
            np.concatenate([f(inputs["L0_ffn_W2"][l]),
                            np.zeros((DI_PAD - DI, DM), np.float32)], 0))
        W[f"b2{l}"] = _cols(inputs["L0_ffn_b2"][l], 8)
        W[f"lngb{l}"] = np.ascontiguousarray(np.stack(
            [f(inputs["L0_ffn_ln_g"][l]), f(inputs["L0_ffn_ln_b"][l])]))
        W[f"qs{l}"] = np.ascontiguousarray(
            np.tile(f(inputs["L0_ffn_q_scale"][l]), 2).reshape(128, 1))
        W[f"ks{l}"] = np.ascontiguousarray(
            np.tile(f(inputs["L0_ffn_k_scale"][l]), 2).reshape(128, 1))

    W["identM"] = np.eye(128, dtype=np.float32)
    d = np.arange(128)
    partner = np.where((d % 64) < 32, d + 32, d - 32)
    P = np.zeros((128, 128), np.float32)
    P[partner, d] = 1.0
    W["swapP"] = P
    Em_ = np.zeros((128, 128), np.float32)
    Em_[d % 64, d] = 1.0
    Em_[d % 64 + 64, d] = 1.0
    W["gramE"] = np.ascontiguousarray(Em_)
    Rb = np.zeros((128, 256), np.float32)
    Rb[:64, 128] = 1.0
    Rb[64:, 129] = 1.0
    W["gramR"] = Rb
    msel = np.zeros((16, 128), np.float32)
    msel[d % 16, d] = 1.0
    W["msel"] = msel
    hsel = np.zeros((16, 8 * 128), np.float32)
    for c in range(8):
        for m in range(128):
            hsel[2 * c + (m >= 64), 128 * c + m] = 1.0
    W["hsel"] = hsel
    permT = np.zeros((128, 128), np.float32)
    for n in range(128):
        h, gp = n // 8, n % 8
        permT[16 * gp + h, n] = 1.0
    W["permT"] = permT
    W["ones_row"] = np.ones((1, 128), np.float32)
    onz = np.zeros((2, DM), np.float32)
    onz[0, :] = 1.0
    W["onz"] = onz
    W["negrow"] = np.full((1, NTILE), -1.0, np.float32)
    W["zeros_col"] = np.zeros((128, 1), np.float32)
    W["ones_col"] = np.ones((128, 1), np.float32)
    return W


SBYTES = 205824


class Views:
    def __init__(self, handle):
        self.h = handle
        self.map = {}

    def add(self, name, off, parts, cols, dtype):
        esz = 4 if dtype == F32 else 2
        assert off % 64 == 0 and off + cols * esz <= SBYTES, \
            f"{name}: off={off} end={off + cols * esz} > {SBYTES}"
        self.map[name] = (off, cols, dtype, parts)

    def __call__(self, name):
        off, cols, dtype, parts = self.map[name]
        esz = 4 if dtype == F32 else 2
        ap = self.h[:parts, off // 2:(off + esz * cols) // 2]
        if dtype == F32:
            ap = ap.bitcast(F32)
        return ap


def _static_layout(v):
    off = [0]
    def A(name, parts, cols, dtype):
        v.add(name, off[0], parts, cols, dtype)
        esz = 4 if dtype == F32 else 2
        off[0] = (off[0] + cols * esz + 63) // 64 * 64
    A("xt", 128, NCH * XSTR, F32)
    A("xb", 128, NCH * XSTR, BF16)
    A("rowsF", 128, 2048, F32)
    A("lntmp", 128, NTILE, F32)
    A("zcol", 128, 1, F32)
    A("onecol", 128, 1, F32)
    A("i1024col", 128, 1, BF16)
    A("gbrow", 2, DM, F32)
    A("b2col", 128, 8, F32)
    A("b1acol", 128, KCH, F32)
    A("b1gcol", 128, KCH, F32)
    A("qscol", 128, 1, F32)
    A("kscol", 128, 1, F32)
    A("dcol", 128, 16, F32)
    A("rcol", 128, 16, F32)
    A("nmax", 128, 16, F32)
    return off[0]


def _phase_layout(v, pers_end, phase):
    off = [pers_end]
    def A(name, parts, cols, dtype):
        v.add(name, off[0], parts, cols, dtype)
        esz = 4 if dtype == F32 else 2
        off[0] = (off[0] + cols * esz + 63) // 64 * 64
    if phase == "att":
        A("ident", 128, 128, BF16)
        A("swapP", 128, 128, BF16)
        A("gramR", 128, 256, BF16)
        A("hsel_f", 16, 8 * 128, F32)
        A("wqkv", 128, NCH * 3 * DM, BF16)
        A("wo", 128, NCH * DM, BF16)
        A("masks", 128, 3 * KSPAN, F32)
        A("tabs", 128, 4 * NTILE, F32)
        A("kcache", 128, NCH * KSTR, BF16)
        A("vtc", 128, 6 * 1024, BF16)
        A("prob", 128, 16 * KSPAN, BF16)
        A("ptb", 128, 2 * KSPAN, BF16)
        A("ob", 128, NCH * NTILE, BF16)
        v.add("sq8q", v.map["prob"][0], 128, NCH * NTILE, BF16)
        v.add("sq8k", v.map["ob"][0], 128, NCH * NTILE, BF16)
        A("qe", 128, NCH * NTILE, BF16)
        A("ke", 128, NCH * NTILE, BF16)
        A("ve", 128, NCH * NTILE, BF16)
        A("t2c", 128, NTILE, F32)
        A("t3c", 128, NTILE, F32)
        v.add("stage", v.map["qe"][0], 128, 3072, F32)
        v.add("stage2", v.map["ve"][0], 128, 3072, F32)
    elif phase == "ffn":
        A("w1a", 128, NCH * DI_PAD, BF16)
        A("w1g", 128, NCH * DI_PAD, BF16)
        A("w2", 128, KCH * DM, BF16)
        A("xn", 128, NCH * FTILE, BF16)
        A("hb", 128, KCH * FTILE, BF16)
        A("gel", 128, FTILE, BF16)
        A("apl", 128, FTILE, BF16)
        v.add("sqt", v.map["lntmp"][0], 128, FTILE, BF16)
        v.add("stage", v.map["xn"][0], 128, 1792, F32)
        v.add("stage2", v.map["hb"][0], 128, 1792, F32)
    elif phase == "mla":
        A("ident", 128, 128, BF16)
        A("swapP", 128, 128, BF16)
        A("permT", 128, 128, BF16)
        A("gramE", 128, 128, BF16)
        A("gramR", 128, 256, BF16)
        A("msel_b", 16, 128, BF16)
        A("msel_f", 16, 128, F32)
        A("onesrow", 1, 128, BF16)
        A("rowsB", 16, 1024, BF16)
        A("wq", 128, NCH * DM, BF16)
        A("wdown", 128, NCH * DLAT, BF16)
        A("wupk", 128, 2 * DM, BF16)
        A("wupv", 128, 2 * DM, BF16)
        A("wout", 128, NCH * DM, BF16)
        A("bq_c", 128, 8, F32)
        A("bdown_c", 128, 2, F32)
        A("bupk_c", 128, 8, F32)
        A("bout_c", 128, 8, F32)
        A("bupv_r", 1, DM, BF16)
        A("tab0", 128, 2 * NTILE, F32)
        A("qe", 128, NCH * NTILE, BF16)
        A("ke", 128, NCH * NTILE, BF16)
        A("vdm", 128, NCH * NTILE, BF16)
        A("latb", 128, 2 * NTILE, BF16)
        A("t2c", 128, NTILE, F32)
        A("t3c", 128, NTILE, F32)
        A("bce", 128, NTILE, BF16)
        A("prod", 128, NCH * NTILE, BF16)
        A("stkA", 128, NTILE, F32)
        A("stkB", 128, NTILE, F32)
        A("fold", 128, NTILE, F32)
        A("pnA", 128, NTILE, F32)
        A("pnB", 128, NTILE, F32)
        A("pbA", 128, NTILE, BF16)
        A("pbB", 128, NTILE, BF16)
        A("pnt", 128, 256, BF16)
        A("pvp", 128, 1024, BF16)
        A("otm", 128, 4 * 1024, BF16)
        A("ob", 128, NCH * NTILE, BF16)
        A("sqt", 128, NTILE, BF16)
        v.add("stage", v.map["qe"][0], 128, 3072, F32)
        v.add("stage2", v.map["vdm"][0], 128, 3072, F32)
    elif phase == "fin":
        A("x1t", 128, NCH * NTILE, F32)
        A("sqt", 128, NTILE, BF16)
    assert off[0] <= SBYTES, f"{phase}: layout {off[0]} > {SBYTES}"
    return off[0]


class Ctx:
    def __init__(self, nc, em, D, views, ps, streams, pers_end):
        self.nc, self.em, self.D, self.views, self.ps = nc, em, D, views, ps
        self.streams = streams
        self.pers_end = pers_end
        self.store_ops = []

    def tap(self, tag, ap, reads):
        if tag not in DEBUG_TAPS:
            return
        sh = list(ap.shape)
        dt = ap.dtype
        d = self.nc.dram_tensor(f"dbg_{tag}", sh, dt, kind="ExternalOutput").ap()
        i = self.em.op("sync", lambda eng: eng.dma_start(out=d, in_=ap),
                       reads=reads, tag=f"tap:{tag}")
        self.store_ops.append(i)


def ld_xtile(cx, stream, t0, n, tag, halo=True):
    em = cx.em
    s = cx.streams[stream]
    xt3 = cx.views("xt").rearrange("p (c j) -> p c j", c=NCH)
    xb3 = cx.views("xb").rearrange("p (c j) -> p c j", c=NCH)
    sr = s.rearrange("(c p) t -> p c t", p=128)
    if halo and t0 == 0:
        em.op("vector", lambda eng: eng.memset(xt3[:, :, 0:1], 0.0),
              writes=("xt",), tag=f"{tag}:z")
    j0 = 0 if (halo and t0 > 0) else 1
    tl0 = t0 - 1 if (halo and t0 > 0) else t0
    em.op("sync", lambda eng: eng.dma_start(out=xt3[:, :, j0:n + 1],
                                            in_=sr[:, :, tl0:t0 + n]),
          writes=("xt",), tag=f"{tag}:ld")
    em.op("vector", lambda eng: eng.tensor_copy(xb3[:, :, 0:n + 1], xt3[:, :, 0:n + 1]),
          reads=("xt",), writes=("xb",), tag=f"{tag}:cast")


def store_xtile(cx, stream, t0, n, tag):
    s = cx.streams[stream]
    xt3 = cx.views("xt").rearrange("p (c j) -> p c j", c=NCH)
    sr = s.rearrange("(c p) t -> p c t", p=128)
    i = cx.em.op("sync", lambda eng: eng.dma_start(out=sr[:, :, t0:t0 + n],
                                                   in_=xt3[:, :, 1:n + 1]),
                 reads=("xt",), tag=f"{tag}:st")
    cx.store_ops.append(i)


def wload(cx, dram_name, rows, cols, wname, scale=None, aliases=()):
    em, D = cx.em, cx.D
    warr = cx.views(wname)
    cap = cx.views.map["stage"][1]
    pieces = [(p0, min(cap, cols - p0)) for p0 in range(0, cols, cap)]
    i = 0
    for kc in range(rows // 128):
        for (p0, pw) in pieces:
            st = "stage" if i % 2 == 0 else "stage2"
            i += 1
            stap = cx.views(st)
            src = D[dram_name][128 * kc:128 * (kc + 1), p0:p0 + pw]
            dst = warr[:, kc * cols + p0: kc * cols + p0 + pw]
            em.op("sync", lambda eng, s=stap, src=src, pw=pw: eng.dma_start(
                out=s[:, :pw], in_=src),
                  writes=(st,) + aliases, tag=f"wld:{dram_name}:{kc}.{p0}")
            if scale is None:
                em.op("vector", lambda eng, s=stap, dst=dst, pw=pw: eng.tensor_copy(
                    dst, s[:, :pw]),
                      reads=(st,) + aliases, writes=(wname,), tag=f"wcst:{dram_name}:{kc}.{p0}")
            else:
                em.op("vector", lambda eng, s=stap, dst=dst, pw=pw: eng.tensor_scalar(
                    dst, s[:, :pw], float(scale), None, ALU.mult),
                      reads=(st,) + aliases, writes=(wname,), tag=f"wcst:{dram_name}:{kc}.{p0}")


def ld_direct(cx, dram_name, dst, parts, cols):
    dap = cx.views(dst)[:parts, :cols]
    src = cx.D[dram_name]
    cx.em.op("sync", lambda eng: eng.dma_start(out=dap, in_=src),
             writes=(dst,), tag=f"ldc:{dram_name}")


def ld_bf16(cx, dram_name, dst, parts, cols, aliases=()):
    stap = cx.views("stage")
    dap = cx.views(dst)[:parts, :cols]
    src = cx.D[dram_name]
    cx.em.op("sync", lambda eng: eng.dma_start(out=stap[:parts, :cols], in_=src),
             writes=("stage",) + aliases, tag=f"ldb:{dram_name}")
    cx.em.op("gpsimd", lambda eng: eng.tensor_copy(dap, stap[:parts, :cols]),
             reads=("stage",) + aliases, writes=(dst,), tag=f"cstb:{dram_name}")


def neg_ones_row(cx, tag):
    return


def emit_ln(cx, n, sq_ap_fn, mu_ap_fn, src_reads, xin_ap_fn, xin_reads,
            out_ap_fn, out_writes, tag):
    em, ps = cx.em, cx.ps
    rf = cx.views("rowsF")
    gb = cx.views("gbrow")
    sqv = cx.views("sqt")
    i1024 = cx.views("i1024col")
    tmp = cx.views("lntmp")[:, 0:n]
    for c in range(NCH):
        em.op("scalar", lambda eng, a=sq_ap_fn(c): eng.activation(
            sqv[:, 0:n], a, AF.Square, bias=0.0, scale=1.0),
              reads=src_reads, writes=("sqt", "lntmp"), tag=f"{tag}:sq{c}")
        em.op("tensor", lambda eng, c=c: eng.matmul(
            ps[6][0:1, 0:n], i1024, sqv[:, 0:n],
            start=(c == 0), stop=(c == NCH - 1)),
              reads=("sqt", "i1024col"), writes=("ps6",), tag=f"{tag}:e2{c}")
    for c in range(NCH):
        em.op("tensor", lambda eng, c=c, a=mu_ap_fn(c): eng.matmul(
            ps[7][0:1, 0:n], i1024, a, start=(c == 0), stop=(c == NCH - 1)),
              reads=src_reads + ("i1024col",), writes=("ps7",), tag=f"{tag}:mu{c}")
    em.op("vector", lambda eng: eng.tensor_copy(rf[0:1, 0:n], ps[7][0:1, 0:n]),
          reads=("ps7",), writes=("rowsFn",), tag=f"{tag}:ev")
    em.op("vector", lambda eng: eng.tensor_copy(rf[0:1, 512:512 + n], ps[6][0:1, 0:n]),
          reads=("ps6",), writes=("rowsFn",), tag=f"{tag}:ev2")
    cx.tap("raw_" + tag.replace(".", "_").replace(":", "_"), rf[0:1, 0:1100],
           ("rowsFn",))
    em.op("vector", lambda eng: eng.scalar_tensor_tensor(
        rf[0:1, 1024:1024 + n], rf[0:1, 0:n], -1.0, rf[0:1, 0:n], ALU.mult, ALU.mult),
          reads=("rowsFn",), writes=("rowsFn",), tag=f"{tag}:mu2")
    em.op("vector", lambda eng: eng.tensor_add(
        rf[0:1, 1024:1024 + n], rf[0:1, 512:512 + n], rf[0:1, 1024:1024 + n]),
          reads=("rowsFn",), writes=("rowsFn",), tag=f"{tag}:var")
    em.op("vector", lambda eng: eng.tensor_scalar(
        rf[0:1, 1024:1024 + n], rf[0:1, 1024:1024 + n], 1e-5, None, ALU.add),
          reads=("rowsFn",), writes=("rowsFn",), tag=f"{tag}:eps")
    em.op("scalar", lambda eng: eng.activation(
        rf[0:1, 512:512 + n], rf[0:1, 1024:1024 + n], AF.Sqrt, bias=0.0, scale=1.0),
          reads=("rowsFn",), writes=("rowsFn",), tag=f"{tag}:sd")
    em.op("vector", lambda eng: eng.reciprocal(rf[0:1, 1024:1024 + n], rf[0:1, 512:512 + n]),
          reads=("rowsFn",), writes=("rowsFn",), tag=f"{tag}:rc")
    em.op("vector", lambda eng: eng.tensor_mul(
        rf[0:1, 1536:1536 + n], rf[0:1, 0:n], rf[0:1, 1024:1024 + n]),
          reads=("rowsFn",), writes=("rowsFn",), tag=f"{tag}:mr")
    cx.tap("rows_" + tag.replace(".", "_").replace(":", "_"), rf[0:2, 0:2048],
           ("rowsFn", "negrow1"))
    for c in range(NCH):
        def bc1(eng, c=c):
            return eng.matmul(ps[2][:, 0:n], gb[0:1, 128 * c:128 * c + 128],
                              rf[0:1, 1024:1024 + n], start=True, stop=True)
        em.op("tensor", bc1, reads=("rowsFn", "gbrow"), writes=("ps2",), tag=f"{tag}:bcA{c}")
        def bc2(eng, c=c):
            return eng.matmul(ps[3][:, 0:n], gb[0:2, 128 * c:128 * c + 128],
                              rf[0:2, 1536:1536 + n], start=True, stop=True)
        em.op("tensor", bc2, reads=("rowsFn", "gbrow", "negrow1"), writes=("ps3",),
              tag=f"{tag}:bcB{c}")
        em.op("vector", lambda eng, a=xin_ap_fn(c): eng.tensor_mul(tmp, a, ps[2][:, 0:n]),
              reads=xin_reads + ("ps2",), writes=("lntmp", "sqt"), tag=f"{tag}:ap1c{c}")
        em.op("vector", lambda eng, oa=out_ap_fn(c): eng.tensor_sub(oa, tmp, ps[3][:, 0:n]),
              reads=("lntmp", "ps3"), writes=out_writes, tag=f"{tag}:ap2c{c}")


def load_phase_consts(cx, specs):
    em, v, D = cx.em, cx.views, cx.D
    rf = v("rowsF")
    for nm, dst, parts, cols, isb in specs:
        src = D[nm]
        dap = v(dst)[:parts, :cols]
        if not isb:
            em.op("sync", lambda eng, d=dap, s=src: eng.dma_start(out=d, in_=s),
                  writes=(dst,), tag=f"pc:{nm}")
        else:
            em.op("sync", lambda eng, s=src, p=parts, c=cols: eng.dma_start(
                out=rf[:p, :c], in_=s),
                  writes=("rowsFn",), tag=f"pcl:{nm}")
            em.op("gpsimd", lambda eng, d=dap, p=parts, c=cols: eng.tensor_copy(
                d, rf[:p, :c]),
                  reads=("rowsFn",), writes=(dst,), tag=f"pcc:{nm}")


def emit_consts(cx):
    em, v, D = cx.em, cx.views, cx.D
    for nm, dst in [("zeros_col", "zcol"), ("ones_col", "onecol")]:
        dap = v(dst)
        src = D[nm]
        em.op("sync", lambda eng, d=dap, s=src: eng.dma_start(out=d, in_=s),
              writes=(dst,), tag=f"ld:{nm}")
    oc = v("onecol")
    ic = v("i1024col")
    em.op("gpsimd", lambda eng: eng.tensor_scalar(ic, oc, 1.0 / 1024.0, None, ALU.mult),
          reads=("onecol",), writes=("i1024col",), tag="mk1024")
    rfv = v("rowsF")
    ng = D["negrow"]
    em.op("sync", lambda eng: eng.dma_start(out=rfv[1:2, 1536:1536 + NTILE], in_=ng),
          writes=("negrow1",), tag="ld:negrow")


def emit_attn(cx, l):
    em, ps, D = cx.em, cx.ps, cx.D
    em.barrier(f"att{l}")
    _phase_layout(cx.views, cx.pers_end, "att")
    v = cx.views
    load_phase_consts(cx, [("identM", "ident", 128, 128, True),
                           ("swapP", "swapP", 128, 128, True),
                           ("gramR", "gramR", 128, 256, True),
                           ("hsel", "hsel_f", 16, 8 * 128, False)])
    _AL = ("qe", "ke", "ve")
    wload(cx, f"Wqkv{l}", DM, 3 * DM, "wqkv", aliases=_AL)
    wload(cx, f"Wo{l}", DM, DM, "wo", aliases=_AL)
    ld_direct(cx, f"qs{l}", "qscol", 128, 1)
    ld_direct(cx, f"ks{l}", "kscol", 128, 1)
    masksv = v("masks")
    em.op("sync", lambda eng: eng.dma_start(
        out=masksv.rearrange("p (s j) -> p s j", s=3),
        in_=D["masks"].rearrange("s p j -> p s j")),
          writes=("masks",), tag=f"att{l}:ldmask")
    stream_in = "x1" if l == 0 else "B"
    stream_out = "A"
    kcv = v("kcache")
    vtcv = v("vtc")
    em.op("vector", lambda eng: eng.memset(
        kcv.rearrange("p (c j) -> p c j", c=NCH)[:, :, 0:WIN], 0.0),
          writes=("kcache",), tag=f"att{l}:zk")
    em.op("vector", lambda eng: eng.memset(vtcv[:, 4 * 1024:6 * 1024], 0.0),
          writes=("vtc",), tag=f"att{l}:zv")
    tabsv = v("tabs")
    qev, kev, vev = v("qe"), v("ke"), v("ve")
    t2v, t3v = v("t2c"), v("t3c")
    probv, ptbv, obv = v("prob"), v("ptb"), v("ob")
    wqkv, wo = v("wqkv"), v("wo")
    rf = v("rowsF")
    xtv, xbv = v("xt"), v("xb")
    gR, hS, sw, idn = v("gramR"), v("hsel_f"), v("swapP"), v("ident")
    qsc, ksc = v("qscol"), v("kscol")
    dcv, rcv, nmv = v("dcol"), v("rcol"), v("nmax")

    def _tile(ti, n, t0, prev_n):
        tg = f"a{l}.{ti}"
        nsub = n // 128
        g0 = t0 // 128
        for qi, nm in enumerate(("Cq", "Sq", "Ck", "Sk")):
            src = D[nm][:, t0:t0 + n]
            em.op("sync", lambda eng, qi=qi, src=src: eng.dma_start(
                out=tabsv[:, qi * NTILE:qi * NTILE + n], in_=src),
                  writes=("tabs",), tag=f"{tg}:tab{nm}")
        if ti > 0:
            em.op("gpsimd", lambda eng, pn=prev_n: eng.tensor_copy(
                kcv.rearrange("p (c j) -> p c j", c=NCH)[:, :, 0:WIN],
                kcv.rearrange("p (c j) -> p c j", c=NCH)[:, :, pn:pn + WIN]),
                  reads=("kcache",), writes=("kcache",), tag=f"{tg}:ksh")
        ld_xtile(cx, stream_in, t0, n, tg)

        sq8q, sq8k = v("sq8q"), v("sq8k")
        for mc in range(24):
            bank = mc % 4
            for kc in range(NCH):
                j0 = 0 if kc >= 4 else 1
                rhs = xbv[:, kc * XSTR + j0: kc * XSTR + j0 + n]
                lhsT = wqkv[:, kc * 3 * DM + 128 * mc: kc * 3 * DM + 128 * mc + 128]
                em.op("tensor", lambda eng, l_=lhsT, r_=rhs, f=(kc == 0), s=(kc == NCH - 1),
                      bank=bank: eng.matmul(ps[bank][:, 0:n], l_, r_, start=f, stop=s),
                      reads=("wqkv", "xb"), writes=(f"ps{bank}",), tag=f"{tg}:qkv{mc}.{kc}")
            c = mc % 8
            if mc < 8:
                dst = qev[:, c * NTILE:c * NTILE + n]
                em.op("scalar", lambda eng, d=dst, bank=bank: eng.mul(
                    d, ps[bank][:, 0:n], qsc[:, 0:1]),
                      reads=(f"ps{bank}", "qscol"), writes=("qe",), tag=f"{tg}:evq{c}")
                em.op("scalar", lambda eng, c=c, bank=bank: eng.activation(
                    sq8q[:, c * NTILE:c * NTILE + n], ps[bank][:, 0:n], AF.Square,
                    bias=0.0, scale=1.0),
                      reads=(f"ps{bank}",), writes=("sq8q", "prob"), tag=f"{tg}:sq{mc}")
            elif mc < 16:
                dst = kev[:, c * NTILE:c * NTILE + n]
                em.op("scalar", lambda eng, d=dst, bank=bank: eng.mul(
                    d, ps[bank][:, 0:n], ksc[:, 0:1]),
                      reads=(f"ps{bank}", "kscol"), writes=("ke",), tag=f"{tg}:evk{c}")
                em.op("scalar", lambda eng, c=c, bank=bank: eng.activation(
                    sq8k[:, c * NTILE:c * NTILE + n], ps[bank][:, 0:n], AF.Square,
                    bias=0.0, scale=1.0),
                      reads=(f"ps{bank}",), writes=("sq8k", "ob"), tag=f"{tg}:sq{mc}")
            else:
                dst = vev[:, c * NTILE:c * NTILE + n]
                em.op("scalar", lambda eng, d=dst, bank=bank: eng.copy(d, ps[bank][:, 0:n]),
                      reads=(f"ps{bank}",), writes=("ve",), tag=f"{tg}:evv{c}")
        for c in range(NCH):
            em.op("tensor", lambda eng, c=c: eng.matmul(
                ps[7][:, 0:n], gR[:, 128 - 2 * c:256 - 2 * c],
                sq8q[:, c * NTILE:c * NTILE + n], start=(c == 0), stop=(c == 7)),
                  reads=("sq8q", "prob", "gramR"), writes=("ps7",), tag=f"{tg}:ssqq{c}")
        for c in range(NCH):
            em.op("tensor", lambda eng, c=c: eng.matmul(
                ps[6][:, 0:n], gR[:, 128 - 2 * c:256 - 2 * c],
                sq8k[:, c * NTILE:c * NTILE + n], start=(c == 0), stop=(c == 7)),
                  reads=("sq8k", "ob", "gramR"), writes=("ps6",), tag=f"{tg}:ssqk{c}")

        for side, sb, ro, qk8 in (("q", 7, 0, 8.0), ("k", 6, 512, 1.0)):
            em.op("scalar", lambda eng, sb=sb, ro=ro: eng.activation(
                rf[0:16, ro:ro + n], ps[sb][0:16, 0:n], AF.Sqrt, bias=0.0, scale=1.0),
                  reads=(f"ps{sb}",), writes=("rowsFn",), tag=f"{tg}:nr{side}")
            em.op("vector", lambda eng, ro=ro: eng.tensor_scalar(
                rf[0:16, ro:ro + n], rf[0:16, ro:ro + n], 1e-12, None, ALU.max),
                  reads=("rowsFn",), writes=("rowsFn",), tag=f"{tg}:cl{side}")
            em.op("vector", lambda eng, ro=ro: eng.reciprocal(
                rf[0:16, 1024 + ro:1024 + ro + n], rf[0:16, ro:ro + n]),
                  reads=("rowsFn",), writes=("rowsFn",), tag=f"{tg}:rc{side}")
            if qk8 != 1.0:
                em.op("vector", lambda eng, ro=ro, qk8=qk8: eng.tensor_scalar(
                    rf[0:16, 1024 + ro:1024 + ro + n], rf[0:16, 1024 + ro:1024 + ro + n],
                    qk8, None, ALU.mult),
                      reads=("rowsFn",), writes=("rowsFn",), tag=f"{tg}:tb{side}")

        for side, srcv, sname, tc, ts, ro in (("q", qev, "qe", 0, 1, 0),
                                              ("k", kev, "ke", 2, 3, 512)):
            for c in range(NCH):
                swb = 4 if c % 2 == 0 else 5
                srcc = srcv[:, c * NTILE:c * NTILE + n]
                em.op("tensor", lambda eng, s=srcc, swb=swb: eng.matmul(
                    ps[swb][:, 0:n], sw, s, start=True, stop=True),
                      reads=(sname, "swapP"), writes=(f"ps{swb}",), tag=f"{tg}:sw{side}{c}")
                em.op("vector", lambda eng, swb=swb, ts=ts: eng.tensor_mul(
                    t2v[:, 0:n], tabsv[:, ts * NTILE:ts * NTILE + n], ps[swb][:, 0:n]),
                      reads=(f"ps{swb}", "tabs"), writes=("t2c",), tag=f"{tg}:t2{side}{c}")
                em.op("gpsimd", lambda eng, s=srcc, tc=tc: eng.tensor_mul(
                    t3v[:, 0:n], s, tabsv[:, tc * NTILE:tc * NTILE + n]),
                      reads=(sname, "tabs"), writes=("t3c",), tag=f"{tg}:t3{side}{c}")
                em.op("vector", lambda eng: eng.tensor_add(
                    t2v[:, 0:n], t2v[:, 0:n], t3v[:, 0:n]),
                      reads=("t2c", "t3c"), writes=("t2c",), tag=f"{tg}:cb{side}{c}")
                em.op("tensor", lambda eng, c=c, ro=ro: eng.matmul(
                    ps[2][:, 0:n], hS[:, 128 * c:128 * c + 128],
                    rf[0:16, 1024 + ro:1024 + ro + n], start=True, stop=True),
                      reads=("rowsFn", "hsel_f"), writes=("ps2",), tag=f"{tg}:bc{side}{c}")
                if side == "q":
                    dst = qev[:, c * NTILE:c * NTILE + n]
                    em.op("vector", lambda eng, d=dst: eng.tensor_mul(
                        d, t2v[:, 0:n], ps[2][:, 0:n]),
                          reads=("t2c", "ps2"), writes=("qe",), tag=f"{tg}:fr{side}{c}")
                else:
                    dst = kcv[:, c * KSTR + WIN: c * KSTR + WIN + n]
                    em.op("vector", lambda eng, d=dst: eng.tensor_mul(
                        d, t2v[:, 0:n], ps[2][:, 0:n]),
                          reads=("t2c", "ps2"), writes=("kcache",), tag=f"{tg}:fr{side}{c}")

        for kc in range(nsub):
            slot = (g0 + kc) % 6
            for h in range(NH):
                b0 = 64 * (h % 2)
                lhsT = vev[b0:b0 + 64,
                           (h // 2) * NTILE + 128 * kc:(h // 2) * NTILE + 128 * kc + 128]
                dst = vtcv[:, slot * 1024 + 64 * h: slot * 1024 + 64 * h + 64]
                idb = idn[b0:b0 + 64, b0:b0 + 64]
                em.op("tensor", lambda eng, l_=lhsT, idb=idb: eng.matmul(
                    ps[3][:, 0:64], l_, idb, start=True, stop=True),
                      reads=("ve", "ident"), writes=("ps3",), tag=f"{tg}:vt{kc}.{h}")
                em.op("scalar", lambda eng, d=dst: eng.copy(d, ps[3][:, 0:64]),
                      reads=("ps3",), writes=("vtc",), tag=f"{tg}:vte{kc}.{h}")

        for s in range(nsub):
            g = g0 + s
            mslot = min(g, 2)
            for h in range(NH):
                sb = 2 if h % 2 == 0 else 3
                lq = qev[64 * (h % 2):64 * (h % 2) + 64,
                         (h // 2) * NTILE + 128 * s:(h // 2) * NTILE + 128 * s + 128]
                rk = kcv[64 * (h % 2):64 * (h % 2) + 64,
                         (h // 2) * KSTR + 128 * s:(h // 2) * KSTR + 128 * s + KSPAN]
                em.op("tensor", lambda eng, lq=lq, rk=rk, sb=sb: eng.matmul(
                    ps[sb][:, 0:KSPAN], lq, rk, start=True, stop=True),
                      reads=("qe", "kcache"), writes=(f"ps{sb}",), tag=f"{tg}:S{s}.{h}")
                em.op("vector", lambda eng, sb=sb, h=h: eng.reduce_max(
                    nmv[:, h:h + 1], ps[sb][:, 0:KSPAN], axis=AX, negate=True),
                      reads=(f"ps{sb}",), writes=(f"nm{h}",), tag=f"{tg}:mx{s}.{h}")
                em.op("vector", lambda eng, sb=sb, ms=mslot: eng.tensor_add(
                    ps[sb][:, 0:KSPAN], masksv[:, ms * KSPAN:(ms + 1) * KSPAN],
                    ps[sb][:, 0:KSPAN]),
                      reads=(f"ps{sb}", "masks"), writes=(f"ps{sb}",), tag=f"{tg}:mk{s}.{h}")
                em.op("scalar", lambda eng, sb=sb, h=h: eng.activation(
                    probv[:, h * KSPAN:(h + 1) * KSPAN], ps[sb][:, 0:KSPAN], AF.Exp,
                    bias=nmv[:, h:h + 1], scale=1.0, accum_out=dcv[:, h:h + 1]),
                      reads=(f"ps{sb}", f"nm{h}"), writes=("prob", f"dc{h}"),
                      tag=f"{tg}:ex{s}.{h}")
            em.op("vector", lambda eng: eng.reciprocal(rcv[:, 0:16], dcv[:, 0:16]),
                  reads=tuple(f"dc{h}" for h in range(NH)), writes=("rcol",),
                  tag=f"{tg}:rcp{s}")
            def emit_pn_pt(h):
                em.op("vector", lambda eng, h=h: eng.tensor_scalar(
                    probv[:, h * KSPAN:(h + 1) * KSPAN], probv[:, h * KSPAN:(h + 1) * KSPAN],
                    v("zcol")[:, 0:1], rcv[:, h:h + 1], ALU.add, ALU.mult),
                      reads=("prob", "rcol", "zcol"), writes=("prob",), tag=f"{tg}:pn{s}.{h}")
                pb = 4 if h % 2 == 0 else 6
                for jc in range(3):
                    lhsT = probv[:, h * KSPAN + 128 * jc: h * KSPAN + 128 * (jc + 1)]
                    em.op("tensor", lambda eng, l_=lhsT, jc=jc, pb=pb: eng.matmul(
                        ps[pb][:, 128 * jc:128 * (jc + 1)], l_, idn, start=True, stop=True),
                          reads=("prob", "ident"), writes=(f"ps{pb}",),
                          tag=f"{tg}:pt{s}.{h}.{jc}")
                pt_half = "ptb0" if h % 2 == 0 else "ptb1"
                dst = ptbv[:, 0:KSPAN] if h % 2 == 0 else ptbv[:, KSPAN:2 * KSPAN]
                em.op("vector", lambda eng, d=dst, pb=pb: eng.tensor_copy(d, ps[pb][:, 0:KSPAN]),
                      reads=(f"ps{pb}",), writes=(pt_half,), tag=f"{tg}:pte{s}.{h}")
            def emit_pv(h):
                pt_half = "ptb0" if h % 2 == 0 else "ptb1"
                po = 0 if h % 2 == 0 else KSPAN
                for jc in range(3):
                    slot = (g - 2 + jc) % 6
                    lhsT = vtcv[:, slot * 1024 + 64 * h: slot * 1024 + 64 * h + 64]
                    tp = (0, 64) if h % 2 == 1 else None
                    oap = ps[5][64 * (h % 2):64 * (h % 2) + 64, 0:128]
                    em.op("tensor", lambda eng, l_=lhsT, jc=jc, tp=tp, oap=oap, po=po:
                          eng.matmul(oap, l_, ptbv[:, po + 128 * jc:po + 128 * (jc + 1)],
                                     start=(jc == 0), stop=(jc == 2), tile_position=tp),
                          reads=("vtc", pt_half), writes=("ps5",), tag=f"{tg}:pv{s}.{h}.{jc}")
                if h % 2 == 1:
                    c = h // 2
                    dst = obv[:, c * NTILE + 128 * s: c * NTILE + 128 * s + 128]
                    em.op("scalar", lambda eng, d=dst: eng.copy(d, ps[5][:, 0:128]),
                          reads=("ps5",), writes=("ob",), tag=f"{tg}:oev{s}.{h}")
            for h in range(NH):
                emit_pn_pt(h)
                if h >= 1:
                    emit_pv(h - 1)
            emit_pv(NH - 1)

        for mc in range(NCH):
            bank = mc % 2
            for kc in range(NCH):
                lhsT = wo[:, kc * DM + 128 * mc: kc * DM + 128 * mc + 128]
                rhs = obv[:, kc * NTILE:kc * NTILE + n]
                em.op("tensor", lambda eng, l_=lhsT, r_=rhs, f=(kc == 0),
                      s=(kc == NCH - 1), bank=bank: eng.matmul(
                          ps[bank][:, 0:n], l_, r_, start=f, stop=s),
                      reads=("wo", "ob"), writes=(f"ps{bank}",), tag=f"{tg}:wo{mc}.{kc}")
            xa = xtv[:, mc * XSTR + 1: mc * XSTR + 1 + n]
            em.op("vector", lambda eng, xa=xa, bank=bank: eng.tensor_add(
                xa, xa, ps[bank][:, 0:n]),
                  reads=(f"ps{bank}", "xt"), writes=("xt",), tag=f"{tg}:res{mc}")
        store_xtile(cx, stream_out, t0, n, tg)

    t0 = 0
    prev_n = None
    for ti, n in enumerate(TILES):
        _tile(ti, n, t0, prev_n)
        prev_n = n
        t0 += n


def emit_ffn(cx, l):
    em, ps, D = cx.em, cx.ps, cx.D
    em.barrier(f"ffn{l}")
    _phase_layout(cx.views, cx.pers_end, "ffn")
    v = cx.views
    _AL = ("xn", "hb")
    wload(cx, f"W1a{l}", DM, DI_PAD, "w1a", aliases=_AL)
    wload(cx, f"W1g{l}", DM, DI_PAD, "w1g", aliases=_AL)
    wload(cx, f"W2{l}", DI_PAD, DM, "w2", aliases=_AL)
    ld_direct(cx, f"lngb{l}", "gbrow", 2, DM)
    ld_direct(cx, f"b2{l}", "b2col", 128, 8)
    ld_direct(cx, f"b1a{l}", "b1acol", 128, KCH)
    ld_direct(cx, f"b1g{l}", "b1gcol", 128, KCH)
    neg_ones_row(cx, f"ffn{l}")
    stream_in = "A"
    stream_out = "B"
    w1a, w1g, w2 = v("w1a"), v("w1g"), v("w2")
    xnv, hbv, gelv, aplv = v("xn"), v("hb"), v("gel"), v("apl")
    xtv, xbv = v("xt"), v("xb")
    b2c, b1a, b1g = v("b2col"), v("b1acol"), v("b1gcol")
    def _tile(ti, n, t0):
        tg = f"f{l}.{ti}"
        ld_xtile(cx, stream_in, t0, n, tg)
        emit_ln(cx, n,
                sq_ap_fn=lambda c: xbv[:, c * XSTR + (0 if c >= 4 else 1):
                                       c * XSTR + (0 if c >= 4 else 1) + n],
                mu_ap_fn=lambda c: xbv[:, c * XSTR + (0 if c >= 4 else 1):
                                       c * XSTR + (0 if c >= 4 else 1) + n],
                src_reads=("xb",),
                xin_ap_fn=lambda c: xtv[:, c * XSTR + (0 if c >= 4 else 1):
                                        c * XSTR + (0 if c >= 4 else 1) + n],
                xin_reads=("xt",),
                out_ap_fn=lambda c: xnv[:, c * FTILE:c * FTILE + n],
                out_writes=("xn",), tag=tg + ":ln")

        for mc in range(KCH):
            for kc in range(NCH):
                lg = w1g[:, kc * DI_PAD + 128 * mc: kc * DI_PAD + 128 * mc + 128]
                rr = xnv[:, kc * FTILE:kc * FTILE + n]
                em.op("tensor", lambda eng, l_=lg, r_=rr, f=(kc == 0), s=(kc == NCH - 1):
                      eng.matmul(ps[0][:, 0:n], l_, r_, start=f, stop=s),
                      reads=("w1g", "xn"), writes=("ps0",), tag=f"{tg}:g{mc}.{kc}")
            for kc in range(NCH):
                la = w1a[:, kc * DI_PAD + 128 * mc: kc * DI_PAD + 128 * mc + 128]
                rr = xnv[:, kc * FTILE:kc * FTILE + n]
                em.op("tensor", lambda eng, l_=la, r_=rr, f=(kc == 0), s=(kc == NCH - 1):
                      eng.matmul(ps[1][:, 0:n], l_, r_, start=f, stop=s),
                      reads=("w1a", "xn"), writes=("ps1",), tag=f"{tg}:a{mc}.{kc}")
            em.op("scalar", lambda eng, mc=mc: eng.activation(
                gelv[:, 0:n], ps[0][:, 0:n], AF.Gelu, bias=b1g[:, mc:mc + 1], scale=1.0),
                  reads=("ps0", "b1gcol"), writes=("gel",), tag=f"{tg}:gl{mc}")
            em.op("scalar", lambda eng, mc=mc: eng.add(
                aplv[:, 0:n], ps[1][:, 0:n], b1a[:, mc:mc + 1]),
                  reads=("ps1", "b1acol"), writes=("apl",), tag=f"{tg}:ap{mc}")
            em.op("vector", lambda eng, mc=mc: eng.tensor_mul(
                hbv[:, mc * FTILE:mc * FTILE + n], aplv[:, 0:n], gelv[:, 0:n]),
                  reads=("apl", "gel"), writes=("hb",), tag=f"{tg}:h{mc}")

        for mc in range(NCH):
            bank = 2 + mc % 2
            for kc in range(KCH):
                lhsT = w2[:, kc * DM + 128 * mc: kc * DM + 128 * mc + 128]
                rhs = hbv[:, kc * FTILE:kc * FTILE + n]
                em.op("tensor", lambda eng, l_=lhsT, r_=rhs, f=(kc == 0),
                      s=(kc == KCH - 1), bank=bank: eng.matmul(
                          ps[bank][:, 0:n], l_, r_, start=f, stop=s),
                      reads=("w2", "hb"), writes=(f"ps{bank}",), tag=f"{tg}:w2{mc}.{kc}")
            em.op("scalar", lambda eng, mc=mc, bank=bank: eng.add(
                ps[bank][:, 0:n], ps[bank][:, 0:n], b2c[:, mc:mc + 1]),
                  reads=(f"ps{bank}", "b2col"), writes=(f"ps{bank}",), tag=f"{tg}:b2{mc}")
            xa = xtv[:, mc * XSTR + 1: mc * XSTR + 1 + n]
            em.op("vector", lambda eng, xa=xa, bank=bank: eng.tensor_add(
                xa, xa, ps[bank][:, 0:n]),
                  reads=(f"ps{bank}", "xt"), writes=("xt",), tag=f"{tg}:res{mc}")
        store_xtile(cx, stream_out, t0, n, tg)

    t0 = 0
    for ti, n in enumerate(FTILES):
        _tile(ti, n, t0)
        t0 += n


def emit_mla(cx):
    em, ps, D = cx.em, cx.ps, cx.D
    em.barrier("mla")
    _phase_layout(cx.views, cx.pers_end, "mla")
    v = cx.views
    load_phase_consts(cx, [("identM", "ident", 128, 128, True),
                           ("swapP", "swapP", 128, 128, True),
                           ("permT", "permT", 128, 128, True),
                           ("gramE", "gramE", 128, 128, True),
                           ("gramR", "gramR", 128, 256, True),
                           ("msel", "msel_b", 16, 128, True),
                           ("ones_row", "onesrow", 1, 128, True),
                           ("msel", "msel_f", 16, 128, False)])
    _AL = ("qe", "ke", "vdm", "latb", "t2c", "t3c")
    wload(cx, "Wq", DM, DM, "wq", scale=0.125, aliases=_AL)
    wload(cx, "Wdown", DM, DLAT, "wdown", aliases=_AL)
    wload(cx, "Wup_k", DLAT, DM, "wupk", aliases=_AL)
    wload(cx, "Wup_v", DLAT, DM, "wupv", aliases=_AL)
    wload(cx, "Wout", DM, DM, "wout", aliases=_AL)
    ld_direct(cx, "bq", "bq_c", 128, 8)
    ld_direct(cx, "bdown", "bdown_c", 128, 2)
    ld_direct(cx, "bup_k", "bupk_c", 128, 8)
    ld_direct(cx, "bout", "bout_c", 128, 8)
    ld_bf16(cx, "bup_v_row", "bupv_r", 1, DM, aliases=_AL)
    ld_direct(cx, "onz", "gbrow", 2, DM)
    bqc = v("bq_c")
    em.op("gpsimd", lambda eng: eng.tensor_scalar(bqc, bqc, 0.125, None, ALU.mult),
          reads=("bq_c",), writes=("bq_c",), tag="mla:sbq")
    neg_ones_row(cx, "mla")

    wq, wdn, wuk, wuv, wot = v("wq"), v("wdown"), v("wupk"), v("wupv"), v("wout")
    bdc, buk, boc, bvr = v("bdown_c"), v("bupk_c"), v("bout_c"), v("bupv_r")
    tab0 = v("tab0")
    qev, kev, vdm, latv = v("qe"), v("ke"), v("vdm"), v("latb")
    t2v, t3v, bcev, prodv = v("t2c"), v("t3c"), v("bce"), v("prod")
    stkA, stkB, foldv = v("stkA"), v("stkB"), v("fold")
    pnA, pnB, pbA, pbB = v("pnA"), v("pnB"), v("pbA"), v("pbB")
    pntv, pvpv, otmv, obv = v("pnt"), v("pvp"), v("otm"), v("ob")
    gE, gR, msb, msf, sw, idn, onr = (v("gramE"), v("gramR"), v("msel_b"),
                                      v("msel_f"), v("swapP"), v("ident"), v("onesrow"))
    rb, rf = v("rowsB"), v("rowsF")
    xtv, xbv = v("xt"), v("xb")

    def _tile(ti, n, t0):
        tg = f"m.{ti}"
        nsub = n // 128
        for qi, nm in enumerate(("C0", "S0")):
            src = D[nm][:, t0:t0 + n]
            em.op("sync", lambda eng, qi=qi, src=src: eng.dma_start(
                out=tab0[:, qi * NTILE:qi * NTILE + n], in_=src),
                  writes=("tab0",), tag=f"{tg}:t{nm}")
        ld_xtile(cx, "x", t0, n, tg, halo=False)

        def xrhs(kc):
            return xbv[:, kc * XSTR + 1: kc * XSTR + 1 + n]

        for mc in range(NCH):
            bank = mc % 2
            for kc in range(NCH):
                lhsT = wq[:, kc * DM + 128 * mc: kc * DM + 128 * mc + 128]
                em.op("tensor", lambda eng, l_=lhsT, r_=xrhs(kc), f=(kc == 0),
                      s=(kc == NCH - 1), bank=bank: eng.matmul(
                          ps[bank][:, 0:n], l_, r_, start=f, stop=s),
                      reads=("wq", "xb"), writes=(f"ps{bank}",), tag=f"{tg}:q{mc}.{kc}")
            dst = qev[:, mc * NTILE:mc * NTILE + n]
            em.op("scalar", lambda eng, d=dst, mc=mc, bank=bank: eng.add(
                d, ps[bank][:, 0:n], bqc[:, mc:mc + 1]),
                  reads=(f"ps{bank}", "bq_c"), writes=("qe",), tag=f"{tg}:qe{mc}")

        for mc in range(2):
            bank = 2 + mc
            for kc in range(NCH):
                lhsT = wdn[:, kc * DLAT + 128 * mc: kc * DLAT + 128 * mc + 128]
                em.op("tensor", lambda eng, l_=lhsT, r_=xrhs(kc), f=(kc == 0),
                      s=(kc == NCH - 1), bank=bank: eng.matmul(
                          ps[bank][:, 0:n], l_, r_, start=f, stop=s),
                      reads=("wdown", "xb"), writes=(f"ps{bank}",), tag=f"{tg}:l{mc}.{kc}")
            dst = latv[:, mc * NTILE:mc * NTILE + n]
            em.op("scalar", lambda eng, d=dst, mc=mc, bank=bank: eng.add(
                d, ps[bank][:, 0:n], bdc[:, mc:mc + 1]),
                  reads=(f"ps{bank}", "bdown_c"), writes=("latb",), tag=f"{tg}:le{mc}")

        for mc in range(NCH):
            bank = mc % 2
            for kc in range(2):
                lhsT = wuk[:, kc * DM + 128 * mc: kc * DM + 128 * mc + 128]
                rhs = latv[:, kc * NTILE:kc * NTILE + n]
                em.op("tensor", lambda eng, l_=lhsT, r_=rhs, f=(kc == 0), s=(kc == 1),
                      bank=bank: eng.matmul(ps[bank][:, 0:n], l_, r_, start=f, stop=s),
                      reads=("wupk", "latb"), writes=(f"ps{bank}",), tag=f"{tg}:k{mc}.{kc}")
            dst = kev[:, mc * NTILE:mc * NTILE + n]
            em.op("scalar", lambda eng, d=dst, mc=mc, bank=bank: eng.add(
                d, ps[bank][:, 0:n], buk[:, mc:mc + 1]),
                  reads=(f"ps{bank}", "bupk_c"), writes=("ke",), tag=f"{tg}:ke{mc}")

        for s in range(nsub):
            for cg in range(2):
                bank = 2 + cg
                for kc in range(2):
                    lhsT = latv[:, kc * NTILE + 128 * s: kc * NTILE + 128 * s + 128]
                    rhsw = wuv[:, kc * DM + 512 * cg: kc * DM + 512 * cg + 512]
                    em.op("tensor", lambda eng, l_=lhsT, r_=rhsw, f=(kc == 0), bank=bank:
                          eng.matmul(ps[bank][:, 0:512], l_, r_, start=f, stop=False),
                          reads=("latb", "wupv"), writes=(f"ps{bank}",),
                          tag=f"{tg}:v{s}.{cg}.{kc}")
                em.op("tensor", lambda eng, cg=cg, bank=bank: eng.matmul(
                    ps[bank][:, 0:512], onr[0:1, 0:128],
                    bvr[0:1, 512 * cg:512 * cg + 512], start=False, stop=True),
                      reads=("onesrow", "bupv_r"), writes=(f"ps{bank}",),
                      tag=f"{tg}:vb{s}.{cg}")
                dst = vdm[:, (2 * s + cg) * 512:(2 * s + cg) * 512 + 512]
                em.op("scalar", lambda eng, d=dst, bank=bank: eng.copy(d, ps[bank][:, 0:512]),
                      reads=(f"ps{bank}",), writes=("vdm",), tag=f"{tg}:ve{s}.{cg}")

        if ti == 0:
            cx.tap("qpre", qev[:, 0:NCH * NTILE], ("qe",))
            cx.tap("kpre", kev[:, 0:NCH * NTILE], ("ke",))
            cx.tap("lat", latv[:, 0:2 * NTILE], ("latb",))
            cx.tap("vdm", vdm[:, 0:NCH * NTILE], ("vdm",))
        for srcv, sname in ((qev, "qe"), (kev, "ke")):
            for c in range(NCH):
                swb = 4 if c % 2 == 0 else 5
                srcc = srcv[:, c * NTILE:c * NTILE + n]
                em.op("tensor", lambda eng, s=srcc, swb=swb: eng.matmul(
                    ps[swb][:, 0:n], sw, s, start=True, stop=True),
                      reads=(sname, "swapP"), writes=(f"ps{swb}",), tag=f"{tg}:sw{sname}{c}")
                em.op("vector", lambda eng, swb=swb: eng.tensor_mul(
                    t2v[:, 0:n], tab0[:, NTILE:NTILE + n], ps[swb][:, 0:n]),
                      reads=(f"ps{swb}", "tab0"), writes=("t2c",), tag=f"{tg}:t2{sname}{c}")
                em.op("gpsimd", lambda eng, s=srcc: eng.tensor_mul(
                    t3v[:, 0:n], s, tab0[:, 0:n]),
                      reads=(sname, "tab0"), writes=("t3c",), tag=f"{tg}:t3{sname}{c}")
                em.op("vector", lambda eng, s=srcc: eng.tensor_add(
                    s, t2v[:, 0:n], t3v[:, 0:n]),
                      reads=("t2c", "t3c"), writes=(sname,), tag=f"{tg}:cb{sname}{c}")

        if ti == 0:
            cx.tap("qrope", qev[:, 0:NCH * NTILE], ("qe",))
            cx.tap("krope", kev[:, 0:NCH * NTILE], ("ke",))
        for g in range(NH):
            b0 = 64 * (g % 2)
            rk = kev[b0:b0 + 64, (g // 2) * NTILE:(g // 2) * NTILE + n]
            gEs = gE[b0:b0 + 64, :]
            em.op("tensor", lambda eng, rk=rk, gEs=gEs: eng.matmul(
                ps[6][:, 0:n], gEs, rk, start=True, stop=True),
                  reads=("ke", "gramE"), writes=("ps6",), tag=f"{tg}:gb{g}")
            em.op("scalar", lambda eng: eng.copy(bcev[:, 0:n], ps[6][:, 0:n]),
                  reads=("ps6",), writes=("bce",), tag=f"{tg}:gbe{g}")
            em.op("vector", lambda eng: eng.tensor_mul(
                prodv.rearrange("p (c j) -> p c j", c=NCH)[:, :, 0:n],
                qev.rearrange("p (c j) -> p c j", c=NCH)[:, :, 0:n],
                bcev[:, 0:n].unsqueeze(1).broadcast_to([128, NCH, n])),
                  reads=("qe", "bce"), writes=("prod",), tag=f"{tg}:gp{g}")
            bank = 2 if g < 8 else 3
            gpos = g % 8
            for c in range(NCH):
                m0 = 16 * gpos + 2 * c
                lhsT = gR[:, 128 - m0:256 - m0]
                st = (gpos == 0 and c == 0)
                lst = (gpos == 7 and c == NCH - 1)
                rhs = prodv[:, c * NTILE:c * NTILE + n]
                em.op("tensor", lambda eng, l_=lhsT, r_=rhs, st=st, lst=lst, bank=bank:
                      eng.matmul(ps[bank][:, 0:n], l_, r_, start=st, stop=lst),
                      reads=("prod", "gramR"), writes=(f"ps{bank}",), tag=f"{tg}:gr{g}.{c}")
        for bank, dst, dn in ((2, stkA, "stkA"), (3, stkB, "stkB")):
            em.op("scalar", lambda eng, d=dst, bank=bank: eng.copy(d[:, 0:n], ps[bank][:, 0:n]),
                  reads=(f"ps{bank}",), writes=(dn,), tag=f"{tg}:se{bank}")
        if ti == 0:
            cx.tap("stkA", stkA[:, 0:NTILE], ("stkA",))
            cx.tap("stkB", stkB[:, 0:NTILE], ("stkB",))
        em.op("vector", lambda eng: eng.tensor_max(foldv[:, 0:n], stkA[:, 0:n], stkB[:, 0:n]),
              reads=("stkA", "stkB"), writes=("fold",), tag=f"{tg}:f0")
        for half in (64, 32):
            em.op("vector", lambda eng, h=half: eng.tensor_copy(
                pnA[0:h, 0:n], foldv[h:2 * h, 0:n]),
                  reads=("fold",), writes=("pnA",), tag=f"{tg}:fc{half}")
            em.op("vector", lambda eng, h=half: eng.tensor_max(
                foldv[0:h, 0:n], foldv[0:h, 0:n], pnA[0:h, 0:n]),
                  reads=("fold", "pnA"), writes=("fold",), tag=f"{tg}:fm{half}")
        em.op("vector", lambda eng: eng.stream_shuffle(
            pnA[0:32, 0:n], foldv[0:32, 0:n], [16, 17, 18, 19, 20, 21, 22, 23, 24, 25, 26, 27, 28, 29, 30, 31, 0, 1, 2, 3, 4, 5, 6, 7, 8, 9, 10, 11, 12, 13, 14, 15]),
              reads=("fold",), writes=("pnA",), tag=f"{tg}:fsh")
        em.op("vector", lambda eng: eng.tensor_max(
            foldv[0:16, 0:n], foldv[0:16, 0:n], pnA[0:16, 0:n]),
              reads=("fold", "pnA"), writes=("fold",), tag=f"{tg}:fm16")
        em.op("vector", lambda eng: eng.tensor_copy(rb[0:16, 0:n], foldv[0:16, 0:n]),
              reads=("fold",), writes=("rowsB",), tag=f"{tg}:mxb")
        em.op("tensor", lambda eng: eng.matmul(ps[6][:, 0:n], msb, rb[0:16, 0:n],
                                               start=True, stop=True),
              reads=("rowsB", "msel_b"), writes=("ps6",), tag=f"{tg}:mbc")
        for srcb, dstp, sn, dn in ((stkA, pnA, "stkA", "pnA"), (stkB, pnB, "stkB", "pnB")):
            em.op("vector", lambda eng, s=srcb, d=dstp: eng.tensor_sub(
                d[:, 0:n], s[:, 0:n], ps[6][:, 0:n]),
                  reads=(sn, "ps6"), writes=(dn,), tag=f"{tg}:sb{sn}")
            em.op("scalar", lambda eng, d=dstp: eng.activation(
                d[:, 0:n], d[:, 0:n], AF.Exp, bias=0.0, scale=1.0),
                  reads=(dn,), writes=(dn,), tag=f"{tg}:ex{sn}")
        em.op("vector", lambda eng: eng.tensor_add(foldv[:, 0:n], pnA[:, 0:n], pnB[:, 0:n]),
              reads=("pnA", "pnB"), writes=("fold",), tag=f"{tg}:d0")
        for half in (64, 32):
            em.op("vector", lambda eng, h=half: eng.tensor_copy(
                stkA[0:h, 0:n], foldv[h:2 * h, 0:n]),
                  reads=("fold",), writes=("stkA",), tag=f"{tg}:dc{half}")
            em.op("vector", lambda eng, h=half: eng.tensor_add(
                foldv[0:h, 0:n], foldv[0:h, 0:n], stkA[0:h, 0:n]),
                  reads=("fold", "stkA"), writes=("fold",), tag=f"{tg}:da{half}")
        em.op("vector", lambda eng: eng.stream_shuffle(
            stkA[0:32, 0:n], foldv[0:32, 0:n], [16, 17, 18, 19, 20, 21, 22, 23, 24, 25, 26, 27, 28, 29, 30, 31, 0, 1, 2, 3, 4, 5, 6, 7, 8, 9, 10, 11, 12, 13, 14, 15]),
              reads=("fold",), writes=("stkA",), tag=f"{tg}:dsh")
        em.op("vector", lambda eng: eng.tensor_add(
            foldv[0:16, 0:n], foldv[0:16, 0:n], stkA[0:16, 0:n]),
              reads=("fold", "stkA"), writes=("fold",), tag=f"{tg}:da16")
        em.op("vector", lambda eng: eng.reciprocal(foldv[0:16, 0:n], foldv[0:16, 0:n]),
              reads=("fold",), writes=("fold",), tag=f"{tg}:drc")
        em.op("tensor", lambda eng: eng.matmul(ps[6][:, 0:n], msf, foldv[0:16, 0:n],
                                               start=True, stop=True),
              reads=("fold", "msel_f"), writes=("ps6",), tag=f"{tg}:rbc")
        for srcp, dstb, sn, dn in ((pnA, pbA, "pnA", "pbA"), (pnB, pbB, "pnB", "pbB")):
            em.op("vector", lambda eng, s=srcp, d=dstb: eng.tensor_mul(
                d[:, 0:n], s[:, 0:n], ps[6][:, 0:n]),
                  reads=(sn, "ps6"), writes=(dn,), tag=f"{tg}:pm{sn}")
        if ti == 0:
            cx.tap("pbA", pbA[:, 0:NTILE], ("pbA",))
            cx.tap("pbB", pbB[:, 0:NTILE], ("pbB",))
        for s in range(nsub):
            for tb, srcb, sn in ((0, pbA, "pbA"), (1, pbB, "pbB")):
                lhsT = srcb[:, 128 * s:128 * (s + 1)]
                em.op("tensor", lambda eng, l_=lhsT, tb=tb: eng.matmul(
                    ps[4][:, 128 * tb:128 * (tb + 1)], l_, v("permT"), start=True, stop=True),
                      reads=(sn, "permT"), writes=("ps4",), tag=f"{tg}:pt{s}.{tb}")
            pnt3 = pntv.rearrange("p (h g) -> p h g", g=16)
            em.op("scalar", lambda eng: eng.copy(
                pnt3[:, :, 0:8], ps[4][:, 0:128].rearrange("p (h g) -> p h g", g=8)),
                  reads=("ps4",), writes=("pnt",), tag=f"{tg}:peA{s}")
            em.op("scalar", lambda eng: eng.copy(
                pnt3[:, :, 8:16], ps[4][:, 128:256].rearrange("p (h g) -> p h g", g=8)),
                  reads=("ps4",), writes=("pnt",), tag=f"{tg}:peB{s}")
            for h in range(NH):
                pin = pntv[:, 16 * h:16 * h + 16].unsqueeze(1).broadcast_to([128, DH, 16])
                vin = vdm[:, 1024 * s:1024 * (s + 1)].rearrange("p (d g) -> p d g", d=DH)
                em.op("vector", lambda eng, pin=pin, vin=vin: eng.tensor_mul(
                    pvpv.rearrange("p (d g) -> p d g", d=DH), pin, vin),
                      reads=("pnt", "vdm"), writes=("pvp",), tag=f"{tg}:pp{s}.{h}")
                oslice = otmv[:, s * 1024 + 64 * h: s * 1024 + 64 * h + 64]
                def _pr(eng, o=oslice):
                    with cx.nc.allow_low_precision(reason="bf16 matmul operand"):
                        return eng.reduce_sum(o, pvpv.rearrange("p (d g) -> p d g", d=DH),
                                              axis=AX)
                em.op("vector", _pr, reads=("pvp",), writes=("otm",), tag=f"{tg}:pr{s}.{h}")
            for c in range(NCH):
                lhsT = otmv[:, s * 1024 + 128 * c: s * 1024 + 128 * (c + 1)]
                dst = obv[:, c * NTILE + 128 * s: c * NTILE + 128 * s + 128]
                em.op("tensor", lambda eng, l_=lhsT: eng.matmul(
                    ps[5][:, 0:128], l_, idn, start=True, stop=True),
                      reads=("otm", "ident"), writes=("ps5",), tag=f"{tg}:ot{s}.{c}")
                em.op("scalar", lambda eng, d=dst: eng.copy(d, ps[5][:, 0:128]),
                      reads=("ps5",), writes=("ob",), tag=f"{tg}:oe{s}.{c}")

        if ti == 0:
            cx.tap("otm", otmv[:, 0:4 * 1024], ("otm",))
            cx.tap("obm", obv[:, 0:NCH * NTILE], ("ob",))
        for mc in range(NCH):
            bank = mc % 2
            for kc in range(NCH):
                lhsT = wot[:, kc * DM + 128 * mc: kc * DM + 128 * mc + 128]
                rhs = obv[:, kc * NTILE:kc * NTILE + n]
                em.op("tensor", lambda eng, l_=lhsT, r_=rhs, f=(kc == 0),
                      s=(kc == NCH - 1), bank=bank: eng.matmul(
                          ps[bank][:, 0:n], l_, r_, start=f, stop=s),
                      reads=("wout", "ob"), writes=(f"ps{bank}",), tag=f"{tg}:wo{mc}.{kc}")
            em.op("scalar", lambda eng, mc=mc, bank=bank: eng.add(
                ps[bank][:, 0:n], ps[bank][:, 0:n], boc[:, mc:mc + 1]),
                  reads=(f"ps{bank}", "bout_c"), writes=(f"ps{bank}",), tag=f"{tg}:bo{mc}")
            xa = xtv[:, mc * XSTR + 1: mc * XSTR + 1 + n]
            em.op("vector", lambda eng, xa=xa, bank=bank: eng.tensor_add(
                xa, xa, ps[bank][:, 0:n]),
                  reads=(f"ps{bank}", "xt"), writes=("xt",), tag=f"{tg}:res{mc}")
        if ti == 0:
            cx.tap("xres", xtv[:, 0:NCH * XSTR], ("xt",))
        xt3 = xtv.rearrange("p (c j) -> p c j", c=NCH)
        xb3 = xbv.rearrange("p (c j) -> p c j", c=NCH)
        em.op("vector", lambda eng: eng.tensor_copy(xb3[:, :, 1:n + 1], xt3[:, :, 1:n + 1]),
              reads=("xt",), writes=("xb",), tag=f"{tg}:reb")
        if ti == 0:
            cx.tap("xbreb", xbv[:, 0:NCH * XSTR], ("xb",))
        emit_ln(cx, n,
                sq_ap_fn=lambda c: xbv[:, c * XSTR + 1: c * XSTR + 1 + n],
                mu_ap_fn=lambda c: xbv[:, c * XSTR + 1: c * XSTR + 1 + n],
                src_reads=("xb",),
                xin_ap_fn=lambda c: xtv[:, c * XSTR + 1: c * XSTR + 1 + n],
                xin_reads=("xt",),
                out_ap_fn=lambda c: xtv[:, c * XSTR + 1: c * XSTR + 1 + n],
                out_writes=("xt",), tag=tg + ":oln")
        store_xtile(cx, "x1", t0, n, tg)

    t0 = 0
    for ti, n in enumerate(TILES):
        _tile(ti, n, t0)
        t0 += n


def emit_final(cx):
    em, ps = cx.em, cx.ps
    em.barrier("fin")
    _phase_layout(cx.views, cx.pers_end, "fin")
    v = cx.views
    neg_ones_row(cx, "fin")
    ld_direct(cx, "onz", "gbrow", 2, DM)
    stream_in = "B"
    x1v = v("x1t")
    xtv, xbv = v("xt"), v("xb")
    x1sr = cx.streams["x1"].rearrange("(c p) t -> p c t", p=128)
    def _tile(ti, n, t0):
        tg = f"z.{ti}"
        ld_xtile(cx, stream_in, t0, n, tg, halo=False)
        em.op("sync", lambda eng, t0=t0: eng.dma_start(
            out=x1v.rearrange("p (c j) -> p c j", c=NCH)[:, :, 0:n],
            in_=x1sr[:, :, t0:t0 + n]),
              writes=("x1t",), tag=f"{tg}:lx1")
        for c in range(NCH):
            xa = xtv[:, c * XSTR + 1: c * XSTR + 1 + n]
            x1a = x1v[:, c * NTILE:c * NTILE + n]
            em.op("vector", lambda eng, xa=xa, x1a=x1a: eng.tensor_add(xa, xa, x1a),
                  reads=("x1t", "xt"), writes=("xt",), tag=f"{tg}:ad{c}")
        xt3 = xtv.rearrange("p (c j) -> p c j", c=NCH)
        xb3 = xbv.rearrange("p (c j) -> p c j", c=NCH)
        em.op("vector", lambda eng: eng.tensor_copy(xb3[:, :, 1:n + 1], xt3[:, :, 1:n + 1]),
              reads=("xt",), writes=("xb",), tag=f"{tg}:reb")
        emit_ln(cx, n,
                sq_ap_fn=lambda c: xbv[:, c * XSTR + 1: c * XSTR + 1 + n],
                mu_ap_fn=lambda c: xbv[:, c * XSTR + 1: c * XSTR + 1 + n],
                src_reads=("xb",),
                xin_ap_fn=lambda c: xtv[:, c * XSTR + 1: c * XSTR + 1 + n],
                xin_reads=("xt",),
                out_ap_fn=lambda c: xtv[:, c * XSTR + 1: c * XSTR + 1 + n],
                out_writes=("xt",), tag=tg + ":fln")
        store_xtile(cx, "out", t0, n, tg)

    t0 = 0
    for ti, n in enumerate(TILES):
        _tile(ti, n, t0)
        t0 += n


def build_program():
    nc = bass.Bass("TRN2", target_bir_lowering=False, detect_race_conditions=False)
    em = Em()
    D = {}
    def din(name, shape, dtype=F32):
        D[name] = nc.dram_tensor(name, list(shape), dtype, kind="ExternalInput").ap()
    din("x_fm", (DM, T_CORE))
    for nm in ("C0", "S0", "Cq", "Sq", "Ck", "Sk"):
        din(nm, (128, T_CORE))
    din("masks", (3, 128, KSPAN))
    for nm, sh in [("Wq", (DM, DM)), ("bq", (128, 8)), ("Wdown", (DM, DLAT)),
                   ("bdown", (128, 2)), ("Wup_k", (DLAT, DM)), ("Wup_v", (DLAT, DM)),
                   ("bup_k", (128, 8)), ("bup_v_row", (1, DM)), ("Wout", (DM, DM)),
                   ("bout", (128, 8)), ("identM", (128, 128)), ("swapP", (128, 128)),
                   ("gramE", (128, 128)), ("gramR", (128, 256)), ("msel", (16, 128)),
                   ("hsel", (16, 8 * 128)), ("permT", (128, 128)),
                   ("ones_row", (1, 128)), ("zeros_col", (128, 1)), ("ones_col", (128, 1)),
                   ("onz", (2, DM)), ("negrow", (1, NTILE))]:
        din(nm, sh)
    for l in range(DEPTH):
        din(f"Wqkv{l}", (DM, 3 * DM)); din(f"Wo{l}", (DM, DM))
        din(f"W1a{l}", (DM, DI_PAD)); din(f"W1g{l}", (DM, DI_PAD)); din(f"W2{l}", (DI_PAD, DM))
        din(f"b1a{l}", (128, KCH)); din(f"b1g{l}", (128, KCH))
        din(f"b2{l}", (128, 8)); din(f"lngb{l}", (2, DM))
        din(f"qs{l}", (128, 1)); din(f"ks{l}", (128, 1))
    out = nc.dram_tensor("out_fm", [DM, T_CORE], F32, kind="ExternalOutput").ap()
    _k = "ExternalOutput" if DEBUG_STREAMS else "Internal"
    sA = nc.dram_tensor("sA", [DM, T_CORE], F32, kind=_k).ap()
    sB = nc.dram_tensor("sB", [DM, T_CORE], F32, kind=_k).ap()
    x1s = nc.dram_tensor("x1s", [DM, T_CORE], F32, kind=_k).ap()
    streams = {"x": D["x_fm"], "A": sA, "B": sB, "x1": x1s, "out": out}

    with contextlib.ExitStack() as es:
        E = es.enter_context
        sems = {e: E(nc.semaphore(f"sem_{e}")) for e in Em.ENGINES}
        for i in range(Em.NDMA_SEMS):
            sems[f"dma{i}"] = E(nc.semaphore(f"dsem{i}"))
        base = E(nc.sbuf_tensor("POOL", [128, SBYTES // 2], BF16))
        views = Views(base)
        pers_end = _static_layout(views)
        ps = [E(nc.psum_tensor(f"ps{i}", [128, 512], F32)) for i in range(8)]
        block = E(nc.Block())
        cx = Ctx(nc, em, D, views, ps, streams, pers_end)
        emit_consts(cx)
        for ph in PIPELINE:
            if ph == "mla":
                emit_mla(cx)
            elif ph.startswith("att"):
                emit_attn(cx, int(ph[3:]))
            elif ph.startswith("ffn"):
                emit_ffn(cx, int(ph[3:]))
            elif ph == "fin":
                emit_final(cx)
        em.op("gpsimd", lambda eng: None, extra_deps=cx.store_ops[-8:], tag="final_wait",
              wait_only=True)
        em.finalize_and_emit(block, sems)
    return nc


def kernel(**inputs):
    W = _host_weights(inputs)
    x = np.asarray(inputs["L0_attn_x"], dtype=np.float32)
    in_maps = []
    for b in range(B):
        for half in range(2):
            start = 0 if half == 0 else START1
            im = dict(W)
            im["x_fm"] = np.ascontiguousarray(x[b, start:start + T_CORE].T)
            im.update(_rope_tables(np.arange(start, start + T_CORE)))
            im["masks"] = _masks(first=(half == 0))
            in_maps.append(im)
    nc = build_program()
    res = run_bass_kernel_spmd(nc, in_maps, core_ids=list(range(8)))
    if DEBUG_STREAMS:
        kernel._dbg = res.results
    outp = np.empty((B, T, DM), np.float32)
    for b in range(B):
        outp[b, :T_CORE] = res.results[2 * b]["out_fm"].T
        outp[b, T_CORE:] = res.results[2 * b + 1]["out_fm"].T[T_CORE - (T - T_CORE):]
    return outp


# version 12 (final)
# speedup vs baseline: 1.0183x; 1.0183x over previous
import contextlib
import numpy as np

import concourse.bass as bass
import concourse.mybir as mybir
from concourse.bass_utils import run_bass_kernel_spmd

F32 = mybir.dt.float32
BF16 = mybir.dt.bfloat16
AF = mybir.ActivationFunctionType
ALU = mybir.AluOpType
AX = mybir.AxisListType.X

B, T, DM = 4, 4096, 1024
NH, DH = 16, 64
DLAT = 256
DEPTH = 6
WIN = 256
DI = 2730
DI_PAD = 2816
KCH = DI_PAD // 128
T_CORE = 2944
START1 = T - T_CORE
NTILE = 512
KSPAN = 384
NCH = 8
MASK_NEG = -1.0e30
TILES = [NTILE] * (T_CORE // NTILE) + ([T_CORE % NTILE] if T_CORE % NTILE else [])
FTILE = 448
FTILES = [FTILE] * (T_CORE // FTILE) + ([T_CORE % FTILE] if T_CORE % FTILE else [])
XSTR = NTILE + 1
KSTR = WIN + NTILE

PIPELINE = ["mla"] + [f"{p}{l}" for l in range(DEPTH) for p in ("att", "ffn")] + ["fin"]
DEBUG_STREAMS = False
DEBUG_TAPS = ()


class Op:
    __slots__ = ("idx", "eng", "emit", "dma", "inc", "milestone", "waits", "tag",
                 "deps", "wait_only", "dma_sem")

    def __init__(self, idx, eng, emit, dma, tag):
        self.idx, self.eng, self.emit, self.dma, self.tag = idx, eng, emit, dma, tag
        self.inc = False
        self.milestone = None
        self.waits = []
        self.deps = []


class Em:
    ENGINES = ("sync", "tensor", "vector", "scalar", "gpsimd")
    NDMA_SEMS = 32

    def __init__(self):
        self.ops = []
        self.last_writer = {}
        self.readers = {}
        self.last_on = {e: None for e in self.ENGINES}
        self.last_real = {e: None for e in self.ENGINES}

    def op(self, eng, emit, reads=(), writes=(), extra_deps=(), tag="", wait_only=False):
        dma = (eng == "sync") and not wait_only
        i = len(self.ops)
        o = Op(i, eng, emit, dma, tag)
        o.wait_only = wait_only
        deps = set(x for x in extra_deps if x is not None)
        for r in reads:
            if r in self.last_writer:
                deps.add(self.last_writer[r])
        for w in writes:
            if w in self.last_writer:
                deps.add(self.last_writer[w])
            for rd in self.readers.get(w, ()):
                deps.add(rd)
        for r in reads:
            self.readers.setdefault(r, []).append(i)
        for w in writes:
            self.last_writer[w] = i
            self.readers[w] = []
        o.deps = sorted(d for d in deps if d != i)
        self.ops.append(o)
        self.last_on[eng] = i
        if not wait_only:
            self.last_real[eng] = i
        return i

    def barrier(self, tag="bar"):
        lasts = dict(self.last_real)
        for e in self.ENGINES:
            deps = [lasts[e2] for e2 in self.ENGINES if e2 != e and lasts[e2] is not None]
            self.op(e, lambda eng: None, extra_deps=deps, tag=f"{tag}:{e}", wait_only=True)

    def finalize_and_emit(self, block, sems):
        ops = self.ops
        for o in ops:
            nd = set()
            stack = list(o.deps)
            while stack:
                d = stack.pop()
                if ops[d].wait_only:
                    stack.extend(ops[d].deps)
                else:
                    nd.add(d)
            o.deps = sorted(nd)
        for o in ops:
            for d in o.deps:
                if ops[d].eng != o.eng:
                    ops[d].inc = True
        counts = {e: 0 for e in self.ENGINES}
        dma_counts = [0] * self.NDMA_SEMS
        dma_i = 0
        for o in ops:
            if o.dma:
                o.inc = True
                o.dma_sem = dma_i % self.NDMA_SEMS
                dma_i += 1
                dma_counts[o.dma_sem] += 16
                o.milestone = dma_counts[o.dma_sem]
            else:
                if o.inc:
                    counts[o.eng] += 1
                o.milestone = counts[o.eng]
        for o in ops:
            need = {}
            for d in o.deps:
                p = ops[d]
                if p.dma:
                    k = f"dma{p.dma_sem}"
                    need[k] = max(need.get(k, 0), p.milestone)
                elif p.eng != o.eng:
                    need[p.eng] = max(need.get(p.eng, 0), p.milestone)
            o.waits = sorted(need.items())
        allsrc = list(self.ENGINES) + [f"dma{i}" for i in range(self.NDMA_SEMS)]
        seen = {e: {e2: 0 for e2 in allsrc} for e in self.ENGINES}
        for o in ops:
            w2 = []
            for (src, val) in o.waits:
                if seen[o.eng][src] < val:
                    seen[o.eng][src] = val
                    w2.append((src, val))
            o.waits = w2
        per_eng = {e: [o for o in ops if o.eng == e] for e in self.ENGINES}

        def mkrun(eng_name):
            def run(eng):
                for o in per_eng[eng_name]:
                    for (src, val) in o.waits:
                        eng.wait_ge(sems[src], val)
                    ins = o.emit(eng)
                    if o.inc:
                        assert ins is not None, f"op {o.idx} [{o.tag}] inc w/o instruction"
                        if o.dma:
                            ins.then_inc(sems[f"dma{o.dma_sem}"], 16)
                        else:
                            ins.then_inc(sems[o.eng], 1)
            return run

        block.sync(mkrun("sync"))
        block.tensor(mkrun("tensor"))
        block.vector(mkrun("vector"))
        block.scalar(mkrun("scalar"))
        block.gpsimd(mkrun("gpsimd"))


def _rope_tables(t_abs):
    t64 = t_abs.astype(np.float64)
    j = np.arange(32, dtype=np.float64)
    inv = 10000.0 ** (-(2.0 * j) / 64.0)
    ang = np.outer(t64, inv)
    cos_, sin_ = np.cos(ang), np.sin(ang)
    bsc = (2.0 * j + 0.4 * 64.0) / (1.4 * 64.0)
    power = (t64 - (T // 2)) / (WIN // 2)
    scale = bsc[None, :] ** power[:, None]
    d = np.arange(128) % 64
    jj = d % 32
    sgn = np.where(d < 32, -1.0, 1.0)

    def FM(ct, st):
        return (np.ascontiguousarray(ct[:, jj].T.astype(np.float32)),
                np.ascontiguousarray((st[:, jj].T * sgn[:, None]).astype(np.float32)))

    C0, S0 = FM(cos_, sin_)
    Cq, Sq = FM(cos_ * scale, sin_ * scale)
    Ck, Sk = FM(cos_ / scale, sin_ / scale)
    return {"C0": C0, "S0": S0, "Cq": Cq, "Sq": Sq, "Ck": Ck, "Sk": Sk}


def _masks(first):
    r = np.arange(128)[:, None]
    jx = np.arange(KSPAN)[None, :]
    band = (jx >= r) & (jx <= r + WIN)
    m = np.zeros((3, 128, KSPAN), np.float32)
    for s in range(3):
        vv = band.copy()
        if first and s == 0:
            vv &= jx >= 256
        if first and s == 1:
            vv &= jx >= 128
        m[s][~vv] = MASK_NEG
    return np.ascontiguousarray(m)


def _cols(vec, nch):
    return np.ascontiguousarray(np.asarray(vec, np.float32).reshape(nch, 128).T)


def _host_weights(inputs):
    W = {}
    f = lambda a: np.ascontiguousarray(np.asarray(a, dtype=np.float32))
    W["Wq"] = f(inputs["L0_attn_Wq"])
    W["bq"] = _cols(inputs["L0_attn_bq"], 8)
    W["Wdown"] = f(inputs["L0_attn_Wdown"])
    W["bdown"] = np.ascontiguousarray(
        np.asarray(inputs["L0_attn_bdown"], np.float32).reshape(2, 128).T)
    kv = f(inputs["L0_attn_Wup"]).reshape(DLAT, NH, 2 * DH)
    W["Wup_k"] = np.ascontiguousarray(kv[:, :, :DH].reshape(DLAT, DM))
    W["Wup_v"] = np.ascontiguousarray(
        np.transpose(kv[:, :, DH:], (0, 2, 1)).reshape(DLAT, DM))
    bkv = f(inputs["L0_attn_bup"]).reshape(NH, 2 * DH)
    W["bup_k"] = _cols(bkv[:, :DH].reshape(DM), 8)
    W["bup_v_row"] = np.ascontiguousarray(
        np.transpose(bkv[:, DH:], (1, 0)).reshape(1, DM))
    W["Wout"] = f(inputs["L0_attn_Wout"])
    W["bout"] = _cols(inputs["L0_attn_bout"], 8)

    for l in range(DEPTH):
        W[f"Wqkv{l}"] = f(inputs["L0_ffn_Wqkv"][l])
        W[f"Wo{l}"] = f(inputs["L0_ffn_Wo"][l])
        W1 = f(inputs["L0_ffn_W1"][l])
        b1 = f(inputs["L0_ffn_b1"][l])
        pad = np.zeros((DM, DI_PAD - DI), np.float32)
        W[f"W1a{l}"] = np.ascontiguousarray(np.concatenate([W1[:, :DI], pad], 1))
        W[f"W1g{l}"] = np.ascontiguousarray(np.concatenate([W1[:, DI:], pad], 1))
        zp = np.zeros(DI_PAD - DI, np.float32)
        W[f"b1a{l}"] = _cols(np.concatenate([b1[:DI], zp]), KCH)
        W[f"b1g{l}"] = _cols(np.concatenate([b1[DI:], zp]), KCH)
        W[f"W2{l}"] = np.ascontiguousarray(
            np.concatenate([f(inputs["L0_ffn_W2"][l]),
                            np.zeros((DI_PAD - DI, DM), np.float32)], 0))
        W[f"b2{l}"] = _cols(inputs["L0_ffn_b2"][l], 8)
        W[f"lngb{l}"] = np.ascontiguousarray(np.stack(
            [f(inputs["L0_ffn_ln_g"][l]), f(inputs["L0_ffn_ln_b"][l])]))
        W[f"qs{l}"] = np.ascontiguousarray(
            np.tile(f(inputs["L0_ffn_q_scale"][l]), 2).reshape(128, 1))
        W[f"ks{l}"] = np.ascontiguousarray(
            np.tile(f(inputs["L0_ffn_k_scale"][l]), 2).reshape(128, 1))

    W["identM"] = np.eye(128, dtype=np.float32)
    d = np.arange(128)
    partner = np.where((d % 64) < 32, d + 32, d - 32)
    P = np.zeros((128, 128), np.float32)
    P[partner, d] = 1.0
    W["swapP"] = P
    Em_ = np.zeros((128, 128), np.float32)
    Em_[d % 64, d] = 1.0
    Em_[d % 64 + 64, d] = 1.0
    W["gramE"] = np.ascontiguousarray(Em_)
    Rb = np.zeros((128, 256), np.float32)
    Rb[:64, 128] = 1.0
    Rb[64:, 129] = 1.0
    W["gramR"] = Rb
    msel = np.zeros((16, 128), np.float32)
    msel[d % 16, d] = 1.0
    W["msel"] = msel
    hsel = np.zeros((16, 8 * 128), np.float32)
    for c in range(8):
        for m in range(128):
            hsel[2 * c + (m >= 64), 128 * c + m] = 1.0
    W["hsel"] = hsel
    permT = np.zeros((128, 128), np.float32)
    for n in range(128):
        h, gp = n // 8, n % 8
        permT[16 * gp + h, n] = 1.0
    W["permT"] = permT
    W["ones_row"] = np.ones((1, 128), np.float32)
    onz = np.zeros((2, DM), np.float32)
    onz[0, :] = 1.0
    W["onz"] = onz
    W["negrow"] = np.full((1, NTILE), -1.0, np.float32)
    W["zeros_col"] = np.zeros((128, 1), np.float32)
    W["ones_col"] = np.ones((128, 1), np.float32)
    return W


SBYTES = 205824


class Views:
    def __init__(self, handle):
        self.h = handle
        self.map = {}

    def add(self, name, off, parts, cols, dtype):
        esz = 4 if dtype == F32 else 2
        assert off % 64 == 0 and off + cols * esz <= SBYTES, \
            f"{name}: off={off} end={off + cols * esz} > {SBYTES}"
        self.map[name] = (off, cols, dtype, parts)

    def __call__(self, name):
        off, cols, dtype, parts = self.map[name]
        esz = 4 if dtype == F32 else 2
        ap = self.h[:parts, off // 2:(off + esz * cols) // 2]
        if dtype == F32:
            ap = ap.bitcast(F32)
        return ap


def _static_layout(v):
    off = [0]
    def A(name, parts, cols, dtype):
        v.add(name, off[0], parts, cols, dtype)
        esz = 4 if dtype == F32 else 2
        off[0] = (off[0] + cols * esz + 63) // 64 * 64
    A("xt", 128, NCH * XSTR, F32)
    A("xb", 128, NCH * XSTR, BF16)
    A("rowsF", 128, 2048, F32)
    A("lntmp", 128, NTILE, F32)
    A("zcol", 128, 1, F32)
    A("onecol", 128, 1, F32)
    A("i1024col", 128, 1, BF16)
    A("gbrow", 2, DM, F32)
    A("b2col", 128, 8, F32)
    A("b1acol", 128, KCH, F32)
    A("b1gcol", 128, KCH, F32)
    A("qscol", 128, 1, F32)
    A("kscol", 128, 1, F32)
    A("dcol", 128, 16, F32)
    A("rcol", 128, 16, F32)
    A("nmax", 128, 16, F32)
    return off[0]


def _phase_layout(v, pers_end, phase):
    off = [pers_end]
    def A(name, parts, cols, dtype):
        v.add(name, off[0], parts, cols, dtype)
        esz = 4 if dtype == F32 else 2
        off[0] = (off[0] + cols * esz + 63) // 64 * 64
    if phase == "att":
        A("ident", 128, 128, BF16)
        A("swapP", 128, 128, BF16)
        A("gramR", 128, 256, BF16)
        A("hsel_f", 16, 8 * 128, F32)
        A("wqkv", 128, NCH * 3 * DM, BF16)
        A("wo", 128, NCH * DM, BF16)
        A("masks", 128, 3 * KSPAN, F32)
        A("tabs", 128, 4 * NTILE, F32)
        A("kcache", 128, NCH * KSTR, BF16)
        A("vtc", 128, 6 * 1024, BF16)
        A("prob", 128, 16 * KSPAN, BF16)
        A("ptb", 128, 2 * KSPAN, BF16)
        A("ob", 128, NCH * NTILE, BF16)
        v.add("sq8q", v.map["prob"][0], 128, NCH * NTILE, BF16)
        v.add("sq8k", v.map["ob"][0], 128, NCH * NTILE, BF16)
        A("qe", 128, NCH * NTILE, BF16)
        A("ke", 128, NCH * NTILE, BF16)
        A("ve", 128, NCH * NTILE, BF16)
        A("t2c", 128, NTILE, F32)
        A("t3c", 128, NTILE, F32)
        v.add("stage", v.map["qe"][0], 128, 3072, F32)
        v.add("stage2", v.map["ve"][0], 128, 3072, F32)
    elif phase == "ffn":
        A("w1a", 128, NCH * DI_PAD, BF16)
        A("w1g", 128, NCH * DI_PAD, BF16)
        A("w2", 128, KCH * DM, BF16)
        A("xn", 128, NCH * FTILE, BF16)
        A("hb", 128, KCH * FTILE, BF16)
        A("gel", 128, FTILE, BF16)
        A("apl", 128, FTILE, BF16)
        v.add("sqt", v.map["lntmp"][0], 128, FTILE, BF16)
        v.add("stage", v.map["xn"][0], 128, 1792, F32)
        v.add("stage2", v.map["hb"][0], 128, 1792, F32)
    elif phase == "mla":
        A("ident", 128, 128, BF16)
        A("swapP", 128, 128, BF16)
        A("permT", 128, 128, BF16)
        A("gramE", 128, 128, BF16)
        A("gramR", 128, 256, BF16)
        A("msel_b", 16, 128, BF16)
        A("msel_f", 16, 128, F32)
        A("onesrow", 1, 128, BF16)
        A("rowsB", 16, 1024, BF16)
        A("wq", 128, NCH * DM, BF16)
        A("wdown", 128, NCH * DLAT, BF16)
        A("wupk", 128, 2 * DM, BF16)
        A("wupv", 128, 2 * DM, BF16)
        A("wout", 128, NCH * DM, BF16)
        A("bq_c", 128, 8, F32)
        A("bdown_c", 128, 2, F32)
        A("bupk_c", 128, 8, F32)
        A("bout_c", 128, 8, F32)
        A("bupv_r", 1, DM, BF16)
        A("tab0", 128, 2 * NTILE, F32)
        A("qe", 128, NCH * NTILE, BF16)
        A("ke", 128, NCH * NTILE, BF16)
        A("vdm", 128, NCH * NTILE, BF16)
        A("latb", 128, 2 * NTILE, BF16)
        A("t2c", 128, NTILE, F32)
        A("t3c", 128, NTILE, F32)
        A("bce", 128, NTILE, BF16)
        A("prod", 128, NCH * NTILE, BF16)
        A("stkA", 128, NTILE, F32)
        A("stkB", 128, NTILE, F32)
        A("fold", 128, NTILE, F32)
        A("pnA", 128, NTILE, F32)
        A("pnB", 128, NTILE, F32)
        A("pbA", 128, NTILE, BF16)
        A("pbB", 128, NTILE, BF16)
        A("pnt", 128, 256, BF16)
        A("pvp", 128, 1024, BF16)
        A("otm", 128, 4 * 1024, BF16)
        A("ob", 128, NCH * NTILE, BF16)
        A("sqt", 128, NTILE, BF16)
        v.add("stage", v.map["qe"][0], 128, 3072, F32)
        v.add("stage2", v.map["vdm"][0], 128, 3072, F32)
    elif phase == "fin":
        A("x1t", 128, NCH * NTILE, F32)
        A("sqt", 128, NTILE, BF16)
    assert off[0] <= SBYTES, f"{phase}: layout {off[0]} > {SBYTES}"
    return off[0]


class Ctx:
    def __init__(self, nc, em, D, views, ps, streams, pers_end):
        self.nc, self.em, self.D, self.views, self.ps = nc, em, D, views, ps
        self.streams = streams
        self.pers_end = pers_end
        self.store_ops = []

    def tap(self, tag, ap, reads):
        if tag not in DEBUG_TAPS:
            return
        sh = list(ap.shape)
        dt = ap.dtype
        d = self.nc.dram_tensor(f"dbg_{tag}", sh, dt, kind="ExternalOutput").ap()
        i = self.em.op("sync", lambda eng: eng.dma_start(out=d, in_=ap),
                       reads=reads, tag=f"tap:{tag}")
        self.store_ops.append(i)


def ld_xtile(cx, stream, t0, n, tag, halo=True):
    em = cx.em
    s = cx.streams[stream]
    xt3 = cx.views("xt").rearrange("p (c j) -> p c j", c=NCH)
    xb3 = cx.views("xb").rearrange("p (c j) -> p c j", c=NCH)
    sr = s.rearrange("(c p) t -> p c t", p=128)
    if halo and t0 == 0:
        em.op("vector", lambda eng: eng.memset(xt3[:, :, 0:1], 0.0),
              writes=("xt",), tag=f"{tag}:z")
    j0 = 0 if (halo and t0 > 0) else 1
    tl0 = t0 - 1 if (halo and t0 > 0) else t0
    em.op("sync", lambda eng: eng.dma_start(out=xt3[:, :, j0:n + 1],
                                            in_=sr[:, :, tl0:t0 + n]),
          writes=("xt",), tag=f"{tag}:ld")
    em.op("vector", lambda eng: eng.tensor_copy(xb3[:, :, 0:n + 1], xt3[:, :, 0:n + 1]),
          reads=("xt",), writes=("xb",), tag=f"{tag}:cast")


def store_xtile(cx, stream, t0, n, tag):
    s = cx.streams[stream]
    xt3 = cx.views("xt").rearrange("p (c j) -> p c j", c=NCH)
    sr = s.rearrange("(c p) t -> p c t", p=128)
    i = cx.em.op("sync", lambda eng: eng.dma_start(out=sr[:, :, t0:t0 + n],
                                                   in_=xt3[:, :, 1:n + 1]),
                 reads=("xt",), tag=f"{tag}:st")
    cx.store_ops.append(i)


def wload(cx, dram_name, rows, cols, wname, scale=None, aliases=()):
    em, D = cx.em, cx.D
    warr = cx.views(wname)
    cap = cx.views.map["stage"][1]
    pieces = [(p0, min(cap, cols - p0)) for p0 in range(0, cols, cap)]
    i = 0
    for kc in range(rows // 128):
        for (p0, pw) in pieces:
            st = "stage" if i % 2 == 0 else "stage2"
            i += 1
            stap = cx.views(st)
            src = D[dram_name][128 * kc:128 * (kc + 1), p0:p0 + pw]
            dst = warr[:, kc * cols + p0: kc * cols + p0 + pw]
            em.op("sync", lambda eng, s=stap, src=src, pw=pw: eng.dma_start(
                out=s[:, :pw], in_=src),
                  writes=(st,) + aliases, tag=f"wld:{dram_name}:{kc}.{p0}")
            if scale is None:
                em.op("vector", lambda eng, s=stap, dst=dst, pw=pw: eng.tensor_copy(
                    dst, s[:, :pw]),
                      reads=(st,) + aliases, writes=(wname,), tag=f"wcst:{dram_name}:{kc}.{p0}")
            else:
                em.op("vector", lambda eng, s=stap, dst=dst, pw=pw: eng.tensor_scalar(
                    dst, s[:, :pw], float(scale), None, ALU.mult),
                      reads=(st,) + aliases, writes=(wname,), tag=f"wcst:{dram_name}:{kc}.{p0}")


def ld_direct(cx, dram_name, dst, parts, cols):
    dap = cx.views(dst)[:parts, :cols]
    src = cx.D[dram_name]
    cx.em.op("sync", lambda eng: eng.dma_start(out=dap, in_=src),
             writes=(dst,), tag=f"ldc:{dram_name}")


def ld_bf16(cx, dram_name, dst, parts, cols, aliases=()):
    stap = cx.views("stage")
    dap = cx.views(dst)[:parts, :cols]
    src = cx.D[dram_name]
    cx.em.op("sync", lambda eng: eng.dma_start(out=stap[:parts, :cols], in_=src),
             writes=("stage",) + aliases, tag=f"ldb:{dram_name}")
    cx.em.op("gpsimd", lambda eng: eng.tensor_copy(dap, stap[:parts, :cols]),
             reads=("stage",) + aliases, writes=(dst,), tag=f"cstb:{dram_name}")


def neg_ones_row(cx, tag):
    return


def emit_ln(cx, n, sq_ap_fn, mu_ap_fn, src_reads, xin_ap_fn, xin_reads,
            out_ap_fn, out_writes, tag):
    em, ps = cx.em, cx.ps
    rf = cx.views("rowsF")
    gb = cx.views("gbrow")
    sqv = cx.views("sqt")
    i1024 = cx.views("i1024col")
    tmp = cx.views("lntmp")[:, 0:n]
    for c in range(NCH):
        em.op("scalar", lambda eng, a=sq_ap_fn(c): eng.activation(
            sqv[:, 0:n], a, AF.Square, bias=0.0, scale=1.0),
              reads=src_reads, writes=("sqt", "lntmp"), tag=f"{tag}:sq{c}")
        em.op("tensor", lambda eng, c=c: eng.matmul(
            ps[6][0:1, 0:n], i1024, sqv[:, 0:n],
            start=(c == 0), stop=(c == NCH - 1)),
              reads=("sqt", "i1024col"), writes=("ps6",), tag=f"{tag}:e2{c}")
    for c in range(NCH):
        em.op("tensor", lambda eng, c=c, a=mu_ap_fn(c): eng.matmul(
            ps[7][0:1, 0:n], i1024, a, start=(c == 0), stop=(c == NCH - 1)),
              reads=src_reads + ("i1024col",), writes=("ps7",), tag=f"{tag}:mu{c}")
    em.op("vector", lambda eng: eng.tensor_copy(rf[0:1, 0:n], ps[7][0:1, 0:n]),
          reads=("ps7",), writes=("rowsFn",), tag=f"{tag}:ev")
    em.op("vector", lambda eng: eng.tensor_copy(rf[0:1, 512:512 + n], ps[6][0:1, 0:n]),
          reads=("ps6",), writes=("rowsFn",), tag=f"{tag}:ev2")
    cx.tap("raw_" + tag.replace(".", "_").replace(":", "_"), rf[0:1, 0:1100],
           ("rowsFn",))
    em.op("vector", lambda eng: eng.scalar_tensor_tensor(
        rf[0:1, 1024:1024 + n], rf[0:1, 0:n], -1.0, rf[0:1, 0:n], ALU.mult, ALU.mult),
          reads=("rowsFn",), writes=("rowsFn",), tag=f"{tag}:mu2")
    em.op("vector", lambda eng: eng.tensor_add(
        rf[0:1, 1024:1024 + n], rf[0:1, 512:512 + n], rf[0:1, 1024:1024 + n]),
          reads=("rowsFn",), writes=("rowsFn",), tag=f"{tag}:var")
    em.op("vector", lambda eng: eng.tensor_scalar(
        rf[0:1, 1024:1024 + n], rf[0:1, 1024:1024 + n], 1e-5, None, ALU.add),
          reads=("rowsFn",), writes=("rowsFn",), tag=f"{tag}:eps")
    em.op("scalar", lambda eng: eng.activation(
        rf[0:1, 512:512 + n], rf[0:1, 1024:1024 + n], AF.Sqrt, bias=0.0, scale=1.0),
          reads=("rowsFn",), writes=("rowsFn",), tag=f"{tag}:sd")
    em.op("vector", lambda eng: eng.reciprocal(rf[0:1, 1024:1024 + n], rf[0:1, 512:512 + n]),
          reads=("rowsFn",), writes=("rowsFn",), tag=f"{tag}:rc")
    em.op("vector", lambda eng: eng.tensor_mul(
        rf[0:1, 1536:1536 + n], rf[0:1, 0:n], rf[0:1, 1024:1024 + n]),
          reads=("rowsFn",), writes=("rowsFn",), tag=f"{tag}:mr")
    cx.tap("rows_" + tag.replace(".", "_").replace(":", "_"), rf[0:2, 0:2048],
           ("rowsFn", "negrow1"))
    for c in range(NCH):
        def bc1(eng, c=c):
            return eng.matmul(ps[2][:, 0:n], gb[0:1, 128 * c:128 * c + 128],
                              rf[0:1, 1024:1024 + n], start=True, stop=True)
        em.op("tensor", bc1, reads=("rowsFn", "gbrow"), writes=("ps2",), tag=f"{tag}:bcA{c}")
        def bc2(eng, c=c):
            return eng.matmul(ps[3][:, 0:n], gb[0:2, 128 * c:128 * c + 128],
                              rf[0:2, 1536:1536 + n], start=True, stop=True)
        em.op("tensor", bc2, reads=("rowsFn", "gbrow", "negrow1"), writes=("ps3",),
              tag=f"{tag}:bcB{c}")
        em.op("vector", lambda eng, a=xin_ap_fn(c): eng.tensor_mul(tmp, a, ps[2][:, 0:n]),
              reads=xin_reads + ("ps2",), writes=("lntmp", "sqt"), tag=f"{tag}:ap1c{c}")
        em.op("vector", lambda eng, oa=out_ap_fn(c): eng.tensor_sub(oa, tmp, ps[3][:, 0:n]),
              reads=("lntmp", "ps3"), writes=out_writes, tag=f"{tag}:ap2c{c}")


def load_phase_consts(cx, specs):
    em, v, D = cx.em, cx.views, cx.D
    rf = v("rowsF")
    for nm, dst, parts, cols, isb in specs:
        src = D[nm]
        dap = v(dst)[:parts, :cols]
        if not isb:
            em.op("sync", lambda eng, d=dap, s=src: eng.dma_start(out=d, in_=s),
                  writes=(dst,), tag=f"pc:{nm}")
        else:
            em.op("sync", lambda eng, s=src, p=parts, c=cols: eng.dma_start(
                out=rf[:p, :c], in_=s),
                  writes=("rowsFn",), tag=f"pcl:{nm}")
            em.op("gpsimd", lambda eng, d=dap, p=parts, c=cols: eng.tensor_copy(
                d, rf[:p, :c]),
                  reads=("rowsFn",), writes=(dst,), tag=f"pcc:{nm}")


def emit_consts(cx):
    em, v, D = cx.em, cx.views, cx.D
    for nm, dst in [("zeros_col", "zcol"), ("ones_col", "onecol")]:
        dap = v(dst)
        src = D[nm]
        em.op("sync", lambda eng, d=dap, s=src: eng.dma_start(out=d, in_=s),
              writes=(dst,), tag=f"ld:{nm}")
    oc = v("onecol")
    ic = v("i1024col")
    em.op("gpsimd", lambda eng: eng.tensor_scalar(ic, oc, 1.0 / 1024.0, None, ALU.mult),
          reads=("onecol",), writes=("i1024col",), tag="mk1024")
    rfv = v("rowsF")
    ng = D["negrow"]
    em.op("sync", lambda eng: eng.dma_start(out=rfv[1:2, 1536:1536 + NTILE], in_=ng),
          writes=("negrow1",), tag="ld:negrow")


def emit_attn(cx, l):
    em, ps, D = cx.em, cx.ps, cx.D
    em.barrier(f"att{l}")
    _phase_layout(cx.views, cx.pers_end, "att")
    v = cx.views
    load_phase_consts(cx, [("identM", "ident", 128, 128, True),
                           ("swapP", "swapP", 128, 128, True),
                           ("gramR", "gramR", 128, 256, True),
                           ("hsel", "hsel_f", 16, 8 * 128, False)])
    _AL = ("qe", "ke", "ve")
    wload(cx, f"Wqkv{l}", DM, 3 * DM, "wqkv", aliases=_AL)
    wload(cx, f"Wo{l}", DM, DM, "wo", aliases=_AL)
    ld_direct(cx, f"qs{l}", "qscol", 128, 1)
    ld_direct(cx, f"ks{l}", "kscol", 128, 1)
    masksv = v("masks")
    em.op("sync", lambda eng: eng.dma_start(
        out=masksv.rearrange("p (s j) -> p s j", s=3),
        in_=D["masks"].rearrange("s p j -> p s j")),
          writes=("masks",), tag=f"att{l}:ldmask")
    stream_in = "x1" if l == 0 else "B"
    stream_out = "A"
    kcv = v("kcache")
    vtcv = v("vtc")
    em.op("vector", lambda eng: eng.memset(
        kcv.rearrange("p (c j) -> p c j", c=NCH)[:, :, 0:WIN], 0.0),
          writes=("kcache",), tag=f"att{l}:zk")
    em.op("vector", lambda eng: eng.memset(vtcv[:, 4 * 1024:6 * 1024], 0.0),
          writes=("vtc",), tag=f"att{l}:zv")
    tabsv = v("tabs")
    qev, kev, vev = v("qe"), v("ke"), v("ve")
    t2v, t3v = v("t2c"), v("t3c")
    probv, ptbv, obv = v("prob"), v("ptb"), v("ob")
    wqkv, wo = v("wqkv"), v("wo")
    rf = v("rowsF")
    xtv, xbv = v("xt"), v("xb")
    gR, hS, sw, idn = v("gramR"), v("hsel_f"), v("swapP"), v("ident")
    qsc, ksc = v("qscol"), v("kscol")
    dcv, rcv, nmv = v("dcol"), v("rcol"), v("nmax")

    def _tile(ti, n, t0, prev_n):
        tg = f"a{l}.{ti}"
        nsub = n // 128
        g0 = t0 // 128
        for qi, nm in enumerate(("Cq", "Sq", "Ck", "Sk")):
            src = D[nm][:, t0:t0 + n]
            em.op("sync", lambda eng, qi=qi, src=src: eng.dma_start(
                out=tabsv[:, qi * NTILE:qi * NTILE + n], in_=src),
                  writes=("tabs",), tag=f"{tg}:tab{nm}")
        if ti > 0:
            em.op("gpsimd", lambda eng, pn=prev_n: eng.tensor_copy(
                kcv.rearrange("p (c j) -> p c j", c=NCH)[:, :, 0:WIN],
                kcv.rearrange("p (c j) -> p c j", c=NCH)[:, :, pn:pn + WIN]),
                  reads=("kcache",), writes=("kcache",), tag=f"{tg}:ksh")
        ld_xtile(cx, stream_in, t0, n, tg)

        sq8q, sq8k = v("sq8q"), v("sq8k")
        for mc in range(24):
            bank = mc % 4
            for kc in range(NCH):
                j0 = 0 if kc >= 4 else 1
                rhs = xbv[:, kc * XSTR + j0: kc * XSTR + j0 + n]
                lhsT = wqkv[:, kc * 3 * DM + 128 * mc: kc * 3 * DM + 128 * mc + 128]
                em.op("tensor", lambda eng, l_=lhsT, r_=rhs, f=(kc == 0), s=(kc == NCH - 1),
                      bank=bank: eng.matmul(ps[bank][:, 0:n], l_, r_, start=f, stop=s),
                      reads=("wqkv", "xb"), writes=(f"ps{bank}",), tag=f"{tg}:qkv{mc}.{kc}")
            c = mc % 8
            if mc < 8:
                dst = qev[:, c * NTILE:c * NTILE + n]
                em.op("scalar", lambda eng, d=dst, bank=bank: eng.mul(
                    d, ps[bank][:, 0:n], qsc[:, 0:1]),
                      reads=(f"ps{bank}", "qscol"), writes=("qe",), tag=f"{tg}:evq{c}")
                em.op("scalar", lambda eng, c=c, bank=bank: eng.activation(
                    sq8q[:, c * NTILE:c * NTILE + n], ps[bank][:, 0:n], AF.Square,
                    bias=0.0, scale=1.0),
                      reads=(f"ps{bank}",), writes=("sq8q", "prob"), tag=f"{tg}:sq{mc}")
            elif mc < 16:
                dst = kev[:, c * NTILE:c * NTILE + n]
                em.op("scalar", lambda eng, d=dst, bank=bank: eng.mul(
                    d, ps[bank][:, 0:n], ksc[:, 0:1]),
                      reads=(f"ps{bank}", "kscol"), writes=("ke",), tag=f"{tg}:evk{c}")
                em.op("scalar", lambda eng, c=c, bank=bank: eng.activation(
                    sq8k[:, c * NTILE:c * NTILE + n], ps[bank][:, 0:n], AF.Square,
                    bias=0.0, scale=1.0),
                      reads=(f"ps{bank}",), writes=("sq8k", "ob"), tag=f"{tg}:sq{mc}")
            else:
                dst = vev[:, c * NTILE:c * NTILE + n]
                em.op("scalar", lambda eng, d=dst, bank=bank: eng.copy(d, ps[bank][:, 0:n]),
                      reads=(f"ps{bank}",), writes=("ve",), tag=f"{tg}:evv{c}")
        for c in range(NCH):
            em.op("tensor", lambda eng, c=c: eng.matmul(
                ps[7][:, 0:n], gR[:, 128 - 2 * c:256 - 2 * c],
                sq8q[:, c * NTILE:c * NTILE + n], start=(c == 0), stop=(c == 7)),
                  reads=("sq8q", "prob", "gramR"), writes=("ps7",), tag=f"{tg}:ssqq{c}")
        for c in range(NCH):
            em.op("tensor", lambda eng, c=c: eng.matmul(
                ps[6][:, 0:n], gR[:, 128 - 2 * c:256 - 2 * c],
                sq8k[:, c * NTILE:c * NTILE + n], start=(c == 0), stop=(c == 7)),
                  reads=("sq8k", "ob", "gramR"), writes=("ps6",), tag=f"{tg}:ssqk{c}")

        for side, sb, ro, qk8 in (("q", 7, 0, 8.0), ("k", 6, 512, 1.0)):
            em.op("scalar", lambda eng, sb=sb, ro=ro: eng.activation(
                rf[0:16, ro:ro + n], ps[sb][0:16, 0:n], AF.Sqrt, bias=0.0, scale=1.0),
                  reads=(f"ps{sb}",), writes=("rowsFn",), tag=f"{tg}:nr{side}")
            em.op("vector", lambda eng, ro=ro: eng.tensor_scalar(
                rf[0:16, ro:ro + n], rf[0:16, ro:ro + n], 1e-12, None, ALU.max),
                  reads=("rowsFn",), writes=("rowsFn",), tag=f"{tg}:cl{side}")
            em.op("vector", lambda eng, ro=ro: eng.reciprocal(
                rf[0:16, 1024 + ro:1024 + ro + n], rf[0:16, ro:ro + n]),
                  reads=("rowsFn",), writes=("rowsFn",), tag=f"{tg}:rc{side}")
            if qk8 != 1.0:
                em.op("vector", lambda eng, ro=ro, qk8=qk8: eng.tensor_scalar(
                    rf[0:16, 1024 + ro:1024 + ro + n], rf[0:16, 1024 + ro:1024 + ro + n],
                    qk8, None, ALU.mult),
                      reads=("rowsFn",), writes=("rowsFn",), tag=f"{tg}:tb{side}")

        for side, srcv, sname, tc, ts, ro in (("q", qev, "qe", 0, 1, 0),
                                              ("k", kev, "ke", 2, 3, 512)):
            for c in range(NCH):
                swb = 4 if c % 2 == 0 else 5
                srcc = srcv[:, c * NTILE:c * NTILE + n]
                em.op("tensor", lambda eng, s=srcc, swb=swb: eng.matmul(
                    ps[swb][:, 0:n], sw, s, start=True, stop=True),
                      reads=(sname, "swapP"), writes=(f"ps{swb}",), tag=f"{tg}:sw{side}{c}")
                em.op("vector", lambda eng, swb=swb, ts=ts: eng.tensor_mul(
                    t2v[:, 0:n], tabsv[:, ts * NTILE:ts * NTILE + n], ps[swb][:, 0:n]),
                      reads=(f"ps{swb}", "tabs"), writes=("t2c",), tag=f"{tg}:t2{side}{c}")
                em.op("gpsimd", lambda eng, s=srcc, tc=tc: eng.tensor_mul(
                    t3v[:, 0:n], s, tabsv[:, tc * NTILE:tc * NTILE + n]),
                      reads=(sname, "tabs"), writes=("t3c",), tag=f"{tg}:t3{side}{c}")
                em.op("gpsimd", lambda eng: eng.tensor_add(
                    t2v[:, 0:n], t2v[:, 0:n], t3v[:, 0:n]),
                      reads=("t2c", "t3c"), writes=("t2c",), tag=f"{tg}:cb{side}{c}")
                em.op("tensor", lambda eng, c=c, ro=ro: eng.matmul(
                    ps[2][:, 0:n], hS[:, 128 * c:128 * c + 128],
                    rf[0:16, 1024 + ro:1024 + ro + n], start=True, stop=True),
                      reads=("rowsFn", "hsel_f"), writes=("ps2",), tag=f"{tg}:bc{side}{c}")
                if side == "q":
                    dst = qev[:, c * NTILE:c * NTILE + n]
                    em.op("vector", lambda eng, d=dst: eng.tensor_mul(
                        d, t2v[:, 0:n], ps[2][:, 0:n]),
                          reads=("t2c", "ps2"), writes=("qe",), tag=f"{tg}:fr{side}{c}")
                else:
                    dst = kcv[:, c * KSTR + WIN: c * KSTR + WIN + n]
                    em.op("vector", lambda eng, d=dst: eng.tensor_mul(
                        d, t2v[:, 0:n], ps[2][:, 0:n]),
                          reads=("t2c", "ps2"), writes=("kcache",), tag=f"{tg}:fr{side}{c}")

        for kc in range(nsub):
            slot = (g0 + kc) % 6
            for h in range(NH):
                b0 = 64 * (h % 2)
                lhsT = vev[b0:b0 + 64,
                           (h // 2) * NTILE + 128 * kc:(h // 2) * NTILE + 128 * kc + 128]
                dst = vtcv[:, slot * 1024 + 64 * h: slot * 1024 + 64 * h + 64]
                idb = idn[b0:b0 + 64, b0:b0 + 64]
                em.op("tensor", lambda eng, l_=lhsT, idb=idb: eng.matmul(
                    ps[3][:, 0:64], l_, idb, start=True, stop=True),
                      reads=("ve", "ident"), writes=("ps3",), tag=f"{tg}:vt{kc}.{h}")
                em.op("scalar", lambda eng, d=dst: eng.copy(d, ps[3][:, 0:64]),
                      reads=("ps3",), writes=("vtc",), tag=f"{tg}:vte{kc}.{h}")

        for s in range(nsub):
            g = g0 + s
            mslot = min(g, 2)
            for h in range(NH):
                sb = 2 if h % 2 == 0 else 3
                lq = qev[64 * (h % 2):64 * (h % 2) + 64,
                         (h // 2) * NTILE + 128 * s:(h // 2) * NTILE + 128 * s + 128]
                rk = kcv[64 * (h % 2):64 * (h % 2) + 64,
                         (h // 2) * KSTR + 128 * s:(h // 2) * KSTR + 128 * s + KSPAN]
                em.op("tensor", lambda eng, lq=lq, rk=rk, sb=sb: eng.matmul(
                    ps[sb][:, 0:KSPAN], lq, rk, start=True, stop=True),
                      reads=("qe", "kcache"), writes=(f"ps{sb}",), tag=f"{tg}:S{s}.{h}")
                em.op("vector", lambda eng, sb=sb, h=h: eng.reduce_max(
                    nmv[:, h:h + 1], ps[sb][:, 0:KSPAN], axis=AX, negate=True),
                      reads=(f"ps{sb}",), writes=(f"nm{h}",), tag=f"{tg}:mx{s}.{h}")
                if mslot == 2:
                    em.op("vector", lambda eng, sb=sb, ms=mslot: eng.tensor_add(
                        ps[sb][:, 0:128], masksv[:, ms * KSPAN:ms * KSPAN + 128],
                        ps[sb][:, 0:128]),
                          reads=(f"ps{sb}", "masks"), writes=(f"ps{sb}",), tag=f"{tg}:mk{s}.{h}")
                    em.op("vector", lambda eng, sb=sb, ms=mslot: eng.tensor_add(
                        ps[sb][:, 256:KSPAN], masksv[:, ms * KSPAN + 256:(ms + 1) * KSPAN],
                        ps[sb][:, 256:KSPAN]),
                          reads=(f"ps{sb}", "masks"), writes=(f"ps{sb}",), tag=f"{tg}:mk2{s}.{h}")
                else:
                    em.op("vector", lambda eng, sb=sb, ms=mslot: eng.tensor_add(
                        ps[sb][:, 0:KSPAN], masksv[:, ms * KSPAN:(ms + 1) * KSPAN],
                        ps[sb][:, 0:KSPAN]),
                          reads=(f"ps{sb}", "masks"), writes=(f"ps{sb}",), tag=f"{tg}:mk{s}.{h}")
                em.op("scalar", lambda eng, sb=sb, h=h: eng.activation(
                    probv[:, h * KSPAN:(h + 1) * KSPAN], ps[sb][:, 0:KSPAN], AF.Exp,
                    bias=nmv[:, h:h + 1], scale=1.0, accum_out=dcv[:, h:h + 1]),
                      reads=(f"ps{sb}", f"nm{h}"), writes=("prob", f"dc{h}"),
                      tag=f"{tg}:ex{s}.{h}")
            em.op("vector", lambda eng: eng.reciprocal(rcv[:, 0:16], dcv[:, 0:16]),
                  reads=tuple(f"dc{h}" for h in range(NH)), writes=("rcol",),
                  tag=f"{tg}:rcp{s}")
            def emit_pn_pt(h):
                em.op("vector", lambda eng, h=h: eng.tensor_scalar(
                    probv[:, h * KSPAN:(h + 1) * KSPAN], probv[:, h * KSPAN:(h + 1) * KSPAN],
                    v("zcol")[:, 0:1], rcv[:, h:h + 1], ALU.add, ALU.mult),
                      reads=("prob", "rcol", "zcol"), writes=("prob",), tag=f"{tg}:pn{s}.{h}")
                pb = 4 if h % 2 == 0 else 6
                for jc in range(3):
                    lhsT = probv[:, h * KSPAN + 128 * jc: h * KSPAN + 128 * (jc + 1)]
                    em.op("tensor", lambda eng, l_=lhsT, jc=jc, pb=pb: eng.matmul(
                        ps[pb][:, 128 * jc:128 * (jc + 1)], l_, idn, start=True, stop=True),
                          reads=("prob", "ident"), writes=(f"ps{pb}",),
                          tag=f"{tg}:pt{s}.{h}.{jc}")
                pt_half = "ptb0" if h % 2 == 0 else "ptb1"
                dst = ptbv[:, 0:KSPAN] if h % 2 == 0 else ptbv[:, KSPAN:2 * KSPAN]
                em.op("scalar", lambda eng, d=dst, pb=pb: eng.copy(d, ps[pb][:, 0:KSPAN]),
                      reads=(f"ps{pb}",), writes=(pt_half,), tag=f"{tg}:pte{s}.{h}")
            def emit_pv(h):
                pt_half = "ptb0" if h % 2 == 0 else "ptb1"
                po = 0 if h % 2 == 0 else KSPAN
                for jc in range(3):
                    slot = (g - 2 + jc) % 6
                    lhsT = vtcv[:, slot * 1024 + 64 * h: slot * 1024 + 64 * h + 64]
                    tp = (0, 64) if h % 2 == 1 else None
                    oap = ps[5][64 * (h % 2):64 * (h % 2) + 64, 0:128]
                    em.op("tensor", lambda eng, l_=lhsT, jc=jc, tp=tp, oap=oap, po=po:
                          eng.matmul(oap, l_, ptbv[:, po + 128 * jc:po + 128 * (jc + 1)],
                                     start=(jc == 0), stop=(jc == 2), tile_position=tp),
                          reads=("vtc", pt_half), writes=("ps5",), tag=f"{tg}:pv{s}.{h}.{jc}")
                if h % 2 == 1:
                    c = h // 2
                    dst = obv[:, c * NTILE + 128 * s: c * NTILE + 128 * s + 128]
                    em.op("scalar", lambda eng, d=dst: eng.copy(d, ps[5][:, 0:128]),
                          reads=("ps5",), writes=("ob",), tag=f"{tg}:oev{s}.{h}")
            for h in range(NH):
                emit_pn_pt(h)
                if h >= 1:
                    emit_pv(h - 1)
            emit_pv(NH - 1)

        for mc in range(NCH):
            bank = mc % 2
            for kc in range(NCH):
                lhsT = wo[:, kc * DM + 128 * mc: kc * DM + 128 * mc + 128]
                rhs = obv[:, kc * NTILE:kc * NTILE + n]
                em.op("tensor", lambda eng, l_=lhsT, r_=rhs, f=(kc == 0),
                      s=(kc == NCH - 1), bank=bank: eng.matmul(
                          ps[bank][:, 0:n], l_, r_, start=f, stop=s),
                      reads=("wo", "ob"), writes=(f"ps{bank}",), tag=f"{tg}:wo{mc}.{kc}")
            xa = xtv[:, mc * XSTR + 1: mc * XSTR + 1 + n]
            em.op("vector", lambda eng, xa=xa, bank=bank: eng.tensor_add(
                xa, xa, ps[bank][:, 0:n]),
                  reads=(f"ps{bank}", "xt"), writes=("xt",), tag=f"{tg}:res{mc}")
        store_xtile(cx, stream_out, t0, n, tg)

    t0 = 0
    prev_n = None
    for ti, n in enumerate(TILES):
        _tile(ti, n, t0, prev_n)
        prev_n = n
        t0 += n


def emit_ffn(cx, l):
    em, ps, D = cx.em, cx.ps, cx.D
    em.barrier(f"ffn{l}")
    _phase_layout(cx.views, cx.pers_end, "ffn")
    v = cx.views
    _AL = ("xn", "hb")
    wload(cx, f"W1a{l}", DM, DI_PAD, "w1a", aliases=_AL)
    wload(cx, f"W1g{l}", DM, DI_PAD, "w1g", aliases=_AL)
    wload(cx, f"W2{l}", DI_PAD, DM, "w2", aliases=_AL)
    ld_direct(cx, f"lngb{l}", "gbrow", 2, DM)
    ld_direct(cx, f"b2{l}", "b2col", 128, 8)
    ld_direct(cx, f"b1a{l}", "b1acol", 128, KCH)
    ld_direct(cx, f"b1g{l}", "b1gcol", 128, KCH)
    neg_ones_row(cx, f"ffn{l}")
    stream_in = "A"
    stream_out = "B"
    w1a, w1g, w2 = v("w1a"), v("w1g"), v("w2")
    xnv, hbv, gelv, aplv = v("xn"), v("hb"), v("gel"), v("apl")
    xtv, xbv = v("xt"), v("xb")
    b2c, b1a, b1g = v("b2col"), v("b1acol"), v("b1gcol")
    def _tile(ti, n, t0):
        tg = f"f{l}.{ti}"
        ld_xtile(cx, stream_in, t0, n, tg)
        emit_ln(cx, n,
                sq_ap_fn=lambda c: xbv[:, c * XSTR + (0 if c >= 4 else 1):
                                       c * XSTR + (0 if c >= 4 else 1) + n],
                mu_ap_fn=lambda c: xbv[:, c * XSTR + (0 if c >= 4 else 1):
                                       c * XSTR + (0 if c >= 4 else 1) + n],
                src_reads=("xb",),
                xin_ap_fn=lambda c: xtv[:, c * XSTR + (0 if c >= 4 else 1):
                                        c * XSTR + (0 if c >= 4 else 1) + n],
                xin_reads=("xt",),
                out_ap_fn=lambda c: xnv[:, c * FTILE:c * FTILE + n],
                out_writes=("xn",), tag=tg + ":ln")

        for mc in range(KCH):
            for kc in range(NCH):
                lg = w1g[:, kc * DI_PAD + 128 * mc: kc * DI_PAD + 128 * mc + 128]
                rr = xnv[:, kc * FTILE:kc * FTILE + n]
                em.op("tensor", lambda eng, l_=lg, r_=rr, f=(kc == 0), s=(kc == NCH - 1):
                      eng.matmul(ps[0][:, 0:n], l_, r_, start=f, stop=s),
                      reads=("w1g", "xn"), writes=("ps0",), tag=f"{tg}:g{mc}.{kc}")
            for kc in range(NCH):
                la = w1a[:, kc * DI_PAD + 128 * mc: kc * DI_PAD + 128 * mc + 128]
                rr = xnv[:, kc * FTILE:kc * FTILE + n]
                em.op("tensor", lambda eng, l_=la, r_=rr, f=(kc == 0), s=(kc == NCH - 1):
                      eng.matmul(ps[1][:, 0:n], l_, r_, start=f, stop=s),
                      reads=("w1a", "xn"), writes=("ps1",), tag=f"{tg}:a{mc}.{kc}")
            em.op("scalar", lambda eng, mc=mc: eng.activation(
                gelv[:, 0:n], ps[0][:, 0:n], AF.Gelu, bias=b1g[:, mc:mc + 1], scale=1.0),
                  reads=("ps0", "b1gcol"), writes=("gel",), tag=f"{tg}:gl{mc}")
            em.op("scalar", lambda eng, mc=mc: eng.add(
                aplv[:, 0:n], ps[1][:, 0:n], b1a[:, mc:mc + 1]),
                  reads=("ps1", "b1acol"), writes=("apl",), tag=f"{tg}:ap{mc}")
            em.op("vector", lambda eng, mc=mc: eng.tensor_mul(
                hbv[:, mc * FTILE:mc * FTILE + n], aplv[:, 0:n], gelv[:, 0:n]),
                  reads=("apl", "gel"), writes=("hb",), tag=f"{tg}:h{mc}")

        for mc in range(NCH):
            bank = 2 + mc % 2
            for kc in range(KCH):
                lhsT = w2[:, kc * DM + 128 * mc: kc * DM + 128 * mc + 128]
                rhs = hbv[:, kc * FTILE:kc * FTILE + n]
                em.op("tensor", lambda eng, l_=lhsT, r_=rhs, f=(kc == 0),
                      s=(kc == KCH - 1), bank=bank: eng.matmul(
                          ps[bank][:, 0:n], l_, r_, start=f, stop=s),
                      reads=("w2", "hb"), writes=(f"ps{bank}",), tag=f"{tg}:w2{mc}.{kc}")
            em.op("scalar", lambda eng, mc=mc, bank=bank: eng.add(
                ps[bank][:, 0:n], ps[bank][:, 0:n], b2c[:, mc:mc + 1]),
                  reads=(f"ps{bank}", "b2col"), writes=(f"ps{bank}",), tag=f"{tg}:b2{mc}")
            xa = xtv[:, mc * XSTR + 1: mc * XSTR + 1 + n]
            em.op("vector", lambda eng, xa=xa, bank=bank: eng.tensor_add(
                xa, xa, ps[bank][:, 0:n]),
                  reads=(f"ps{bank}", "xt"), writes=("xt",), tag=f"{tg}:res{mc}")
        store_xtile(cx, stream_out, t0, n, tg)

    t0 = 0
    for ti, n in enumerate(FTILES):
        _tile(ti, n, t0)
        t0 += n


def emit_mla(cx):
    em, ps, D = cx.em, cx.ps, cx.D
    em.barrier("mla")
    _phase_layout(cx.views, cx.pers_end, "mla")
    v = cx.views
    load_phase_consts(cx, [("identM", "ident", 128, 128, True),
                           ("swapP", "swapP", 128, 128, True),
                           ("permT", "permT", 128, 128, True),
                           ("gramE", "gramE", 128, 128, True),
                           ("gramR", "gramR", 128, 256, True),
                           ("msel", "msel_b", 16, 128, True),
                           ("ones_row", "onesrow", 1, 128, True),
                           ("msel", "msel_f", 16, 128, False)])
    _AL = ("qe", "ke", "vdm", "latb", "t2c", "t3c")
    wload(cx, "Wq", DM, DM, "wq", scale=0.125, aliases=_AL)
    wload(cx, "Wdown", DM, DLAT, "wdown", aliases=_AL)
    wload(cx, "Wup_k", DLAT, DM, "wupk", aliases=_AL)
    wload(cx, "Wup_v", DLAT, DM, "wupv", aliases=_AL)
    wload(cx, "Wout", DM, DM, "wout", aliases=_AL)
    ld_direct(cx, "bq", "bq_c", 128, 8)
    ld_direct(cx, "bdown", "bdown_c", 128, 2)
    ld_direct(cx, "bup_k", "bupk_c", 128, 8)
    ld_direct(cx, "bout", "bout_c", 128, 8)
    ld_bf16(cx, "bup_v_row", "bupv_r", 1, DM, aliases=_AL)
    ld_direct(cx, "onz", "gbrow", 2, DM)
    bqc = v("bq_c")
    em.op("gpsimd", lambda eng: eng.tensor_scalar(bqc, bqc, 0.125, None, ALU.mult),
          reads=("bq_c",), writes=("bq_c",), tag="mla:sbq")
    neg_ones_row(cx, "mla")

    wq, wdn, wuk, wuv, wot = v("wq"), v("wdown"), v("wupk"), v("wupv"), v("wout")
    bdc, buk, boc, bvr = v("bdown_c"), v("bupk_c"), v("bout_c"), v("bupv_r")
    tab0 = v("tab0")
    qev, kev, vdm, latv = v("qe"), v("ke"), v("vdm"), v("latb")
    t2v, t3v, bcev, prodv = v("t2c"), v("t3c"), v("bce"), v("prod")
    stkA, stkB, foldv = v("stkA"), v("stkB"), v("fold")
    pnA, pnB, pbA, pbB = v("pnA"), v("pnB"), v("pbA"), v("pbB")
    pntv, pvpv, otmv, obv = v("pnt"), v("pvp"), v("otm"), v("ob")
    gE, gR, msb, msf, sw, idn, onr = (v("gramE"), v("gramR"), v("msel_b"),
                                      v("msel_f"), v("swapP"), v("ident"), v("onesrow"))
    rb, rf = v("rowsB"), v("rowsF")
    xtv, xbv = v("xt"), v("xb")

    def _tile(ti, n, t0):
        tg = f"m.{ti}"
        nsub = n // 128
        for qi, nm in enumerate(("C0", "S0")):
            src = D[nm][:, t0:t0 + n]
            em.op("sync", lambda eng, qi=qi, src=src: eng.dma_start(
                out=tab0[:, qi * NTILE:qi * NTILE + n], in_=src),
                  writes=("tab0",), tag=f"{tg}:t{nm}")
        ld_xtile(cx, "x", t0, n, tg, halo=False)

        def xrhs(kc):
            return xbv[:, kc * XSTR + 1: kc * XSTR + 1 + n]

        for mc in range(NCH):
            bank = mc % 2
            for kc in range(NCH):
                lhsT = wq[:, kc * DM + 128 * mc: kc * DM + 128 * mc + 128]
                em.op("tensor", lambda eng, l_=lhsT, r_=xrhs(kc), f=(kc == 0),
                      s=(kc == NCH - 1), bank=bank: eng.matmul(
                          ps[bank][:, 0:n], l_, r_, start=f, stop=s),
                      reads=("wq", "xb"), writes=(f"ps{bank}",), tag=f"{tg}:q{mc}.{kc}")
            dst = qev[:, mc * NTILE:mc * NTILE + n]
            em.op("scalar", lambda eng, d=dst, mc=mc, bank=bank: eng.add(
                d, ps[bank][:, 0:n], bqc[:, mc:mc + 1]),
                  reads=(f"ps{bank}", "bq_c"), writes=("qe",), tag=f"{tg}:qe{mc}")

        for mc in range(2):
            bank = 2 + mc
            for kc in range(NCH):
                lhsT = wdn[:, kc * DLAT + 128 * mc: kc * DLAT + 128 * mc + 128]
                em.op("tensor", lambda eng, l_=lhsT, r_=xrhs(kc), f=(kc == 0),
                      s=(kc == NCH - 1), bank=bank: eng.matmul(
                          ps[bank][:, 0:n], l_, r_, start=f, stop=s),
                      reads=("wdown", "xb"), writes=(f"ps{bank}",), tag=f"{tg}:l{mc}.{kc}")
            dst = latv[:, mc * NTILE:mc * NTILE + n]
            em.op("scalar", lambda eng, d=dst, mc=mc, bank=bank: eng.add(
                d, ps[bank][:, 0:n], bdc[:, mc:mc + 1]),
                  reads=(f"ps{bank}", "bdown_c"), writes=("latb",), tag=f"{tg}:le{mc}")

        for mc in range(NCH):
            bank = mc % 2
            for kc in range(2):
                lhsT = wuk[:, kc * DM + 128 * mc: kc * DM + 128 * mc + 128]
                rhs = latv[:, kc * NTILE:kc * NTILE + n]
                em.op("tensor", lambda eng, l_=lhsT, r_=rhs, f=(kc == 0), s=(kc == 1),
                      bank=bank: eng.matmul(ps[bank][:, 0:n], l_, r_, start=f, stop=s),
                      reads=("wupk", "latb"), writes=(f"ps{bank}",), tag=f"{tg}:k{mc}.{kc}")
            dst = kev[:, mc * NTILE:mc * NTILE + n]
            em.op("scalar", lambda eng, d=dst, mc=mc, bank=bank: eng.add(
                d, ps[bank][:, 0:n], buk[:, mc:mc + 1]),
                  reads=(f"ps{bank}", "bupk_c"), writes=("ke",), tag=f"{tg}:ke{mc}")

        for s in range(nsub):
            for cg in range(2):
                bank = 2 + cg
                for kc in range(2):
                    lhsT = latv[:, kc * NTILE + 128 * s: kc * NTILE + 128 * s + 128]
                    rhsw = wuv[:, kc * DM + 512 * cg: kc * DM + 512 * cg + 512]
                    em.op("tensor", lambda eng, l_=lhsT, r_=rhsw, f=(kc == 0), bank=bank:
                          eng.matmul(ps[bank][:, 0:512], l_, r_, start=f, stop=False),
                          reads=("latb", "wupv"), writes=(f"ps{bank}",),
                          tag=f"{tg}:v{s}.{cg}.{kc}")
                em.op("tensor", lambda eng, cg=cg, bank=bank: eng.matmul(
                    ps[bank][:, 0:512], onr[0:1, 0:128],
                    bvr[0:1, 512 * cg:512 * cg + 512], start=False, stop=True),
                      reads=("onesrow", "bupv_r"), writes=(f"ps{bank}",),
                      tag=f"{tg}:vb{s}.{cg}")
                dst = vdm[:, (2 * s + cg) * 512:(2 * s + cg) * 512 + 512]
                em.op("scalar", lambda eng, d=dst, bank=bank: eng.copy(d, ps[bank][:, 0:512]),
                      reads=(f"ps{bank}",), writes=("vdm",), tag=f"{tg}:ve{s}.{cg}")

        if ti == 0:
            cx.tap("qpre", qev[:, 0:NCH * NTILE], ("qe",))
            cx.tap("kpre", kev[:, 0:NCH * NTILE], ("ke",))
            cx.tap("lat", latv[:, 0:2 * NTILE], ("latb",))
            cx.tap("vdm", vdm[:, 0:NCH * NTILE], ("vdm",))
        for srcv, sname in ((qev, "qe"), (kev, "ke")):
            for c in range(NCH):
                swb = 4 if c % 2 == 0 else 5
                srcc = srcv[:, c * NTILE:c * NTILE + n]
                em.op("tensor", lambda eng, s=srcc, swb=swb: eng.matmul(
                    ps[swb][:, 0:n], sw, s, start=True, stop=True),
                      reads=(sname, "swapP"), writes=(f"ps{swb}",), tag=f"{tg}:sw{sname}{c}")
                em.op("vector", lambda eng, swb=swb: eng.tensor_mul(
                    t2v[:, 0:n], tab0[:, NTILE:NTILE + n], ps[swb][:, 0:n]),
                      reads=(f"ps{swb}", "tab0"), writes=("t2c",), tag=f"{tg}:t2{sname}{c}")
                em.op("gpsimd", lambda eng, s=srcc: eng.tensor_mul(
                    t3v[:, 0:n], s, tab0[:, 0:n]),
                      reads=(sname, "tab0"), writes=("t3c",), tag=f"{tg}:t3{sname}{c}")
                em.op("vector", lambda eng, s=srcc: eng.tensor_add(
                    s, t2v[:, 0:n], t3v[:, 0:n]),
                      reads=("t2c", "t3c"), writes=(sname,), tag=f"{tg}:cb{sname}{c}")

        if ti == 0:
            cx.tap("qrope", qev[:, 0:NCH * NTILE], ("qe",))
            cx.tap("krope", kev[:, 0:NCH * NTILE], ("ke",))
        for g in range(NH):
            b0 = 64 * (g % 2)
            rk = kev[b0:b0 + 64, (g // 2) * NTILE:(g // 2) * NTILE + n]
            gEs = gE[b0:b0 + 64, :]
            em.op("tensor", lambda eng, rk=rk, gEs=gEs: eng.matmul(
                ps[6][:, 0:n], gEs, rk, start=True, stop=True),
                  reads=("ke", "gramE"), writes=("ps6",), tag=f"{tg}:gb{g}")
            em.op("scalar", lambda eng: eng.copy(bcev[:, 0:n], ps[6][:, 0:n]),
                  reads=("ps6",), writes=("bce",), tag=f"{tg}:gbe{g}")
            em.op("vector", lambda eng: eng.tensor_mul(
                prodv.rearrange("p (c j) -> p c j", c=NCH)[:, :, 0:n],
                qev.rearrange("p (c j) -> p c j", c=NCH)[:, :, 0:n],
                bcev[:, 0:n].unsqueeze(1).broadcast_to([128, NCH, n])),
                  reads=("qe", "bce"), writes=("prod",), tag=f"{tg}:gp{g}")
            bank = 2 if g < 8 else 3
            gpos = g % 8
            for c in range(NCH):
                m0 = 16 * gpos + 2 * c
                lhsT = gR[:, 128 - m0:256 - m0]
                st = (gpos == 0 and c == 0)
                lst = (gpos == 7 and c == NCH - 1)
                rhs = prodv[:, c * NTILE:c * NTILE + n]
                em.op("tensor", lambda eng, l_=lhsT, r_=rhs, st=st, lst=lst, bank=bank:
                      eng.matmul(ps[bank][:, 0:n], l_, r_, start=st, stop=lst),
                      reads=("prod", "gramR"), writes=(f"ps{bank}",), tag=f"{tg}:gr{g}.{c}")
        for bank, dst, dn in ((2, stkA, "stkA"), (3, stkB, "stkB")):
            em.op("scalar", lambda eng, d=dst, bank=bank: eng.copy(d[:, 0:n], ps[bank][:, 0:n]),
                  reads=(f"ps{bank}",), writes=(dn,), tag=f"{tg}:se{bank}")
        if ti == 0:
            cx.tap("stkA", stkA[:, 0:NTILE], ("stkA",))
            cx.tap("stkB", stkB[:, 0:NTILE], ("stkB",))
        em.op("vector", lambda eng: eng.tensor_max(foldv[:, 0:n], stkA[:, 0:n], stkB[:, 0:n]),
              reads=("stkA", "stkB"), writes=("fold",), tag=f"{tg}:f0")
        for half in (64, 32):
            em.op("vector", lambda eng, h=half: eng.tensor_copy(
                pnA[0:h, 0:n], foldv[h:2 * h, 0:n]),
                  reads=("fold",), writes=("pnA",), tag=f"{tg}:fc{half}")
            em.op("vector", lambda eng, h=half: eng.tensor_max(
                foldv[0:h, 0:n], foldv[0:h, 0:n], pnA[0:h, 0:n]),
                  reads=("fold", "pnA"), writes=("fold",), tag=f"{tg}:fm{half}")
        em.op("vector", lambda eng: eng.stream_shuffle(
            pnA[0:32, 0:n], foldv[0:32, 0:n], [16, 17, 18, 19, 20, 21, 22, 23, 24, 25, 26, 27, 28, 29, 30, 31, 0, 1, 2, 3, 4, 5, 6, 7, 8, 9, 10, 11, 12, 13, 14, 15]),
              reads=("fold",), writes=("pnA",), tag=f"{tg}:fsh")
        em.op("vector", lambda eng: eng.tensor_max(
            foldv[0:16, 0:n], foldv[0:16, 0:n], pnA[0:16, 0:n]),
              reads=("fold", "pnA"), writes=("fold",), tag=f"{tg}:fm16")
        em.op("vector", lambda eng: eng.tensor_copy(rb[0:16, 0:n], foldv[0:16, 0:n]),
              reads=("fold",), writes=("rowsB",), tag=f"{tg}:mxb")
        em.op("tensor", lambda eng: eng.matmul(ps[6][:, 0:n], msb, rb[0:16, 0:n],
                                               start=True, stop=True),
              reads=("rowsB", "msel_b"), writes=("ps6",), tag=f"{tg}:mbc")
        for srcb, dstp, sn, dn in ((stkA, pnA, "stkA", "pnA"), (stkB, pnB, "stkB", "pnB")):
            em.op("vector", lambda eng, s=srcb, d=dstp: eng.tensor_sub(
                d[:, 0:n], s[:, 0:n], ps[6][:, 0:n]),
                  reads=(sn, "ps6"), writes=(dn,), tag=f"{tg}:sb{sn}")
            em.op("scalar", lambda eng, d=dstp: eng.activation(
                d[:, 0:n], d[:, 0:n], AF.Exp, bias=0.0, scale=1.0),
                  reads=(dn,), writes=(dn,), tag=f"{tg}:ex{sn}")
        em.op("vector", lambda eng: eng.tensor_add(foldv[:, 0:n], pnA[:, 0:n], pnB[:, 0:n]),
              reads=("pnA", "pnB"), writes=("fold",), tag=f"{tg}:d0")
        for half in (64, 32):
            em.op("vector", lambda eng, h=half: eng.tensor_copy(
                stkA[0:h, 0:n], foldv[h:2 * h, 0:n]),
                  reads=("fold",), writes=("stkA",), tag=f"{tg}:dc{half}")
            em.op("vector", lambda eng, h=half: eng.tensor_add(
                foldv[0:h, 0:n], foldv[0:h, 0:n], stkA[0:h, 0:n]),
                  reads=("fold", "stkA"), writes=("fold",), tag=f"{tg}:da{half}")
        em.op("vector", lambda eng: eng.stream_shuffle(
            stkA[0:32, 0:n], foldv[0:32, 0:n], [16, 17, 18, 19, 20, 21, 22, 23, 24, 25, 26, 27, 28, 29, 30, 31, 0, 1, 2, 3, 4, 5, 6, 7, 8, 9, 10, 11, 12, 13, 14, 15]),
              reads=("fold",), writes=("stkA",), tag=f"{tg}:dsh")
        em.op("vector", lambda eng: eng.tensor_add(
            foldv[0:16, 0:n], foldv[0:16, 0:n], stkA[0:16, 0:n]),
              reads=("fold", "stkA"), writes=("fold",), tag=f"{tg}:da16")
        em.op("vector", lambda eng: eng.reciprocal(foldv[0:16, 0:n], foldv[0:16, 0:n]),
              reads=("fold",), writes=("fold",), tag=f"{tg}:drc")
        em.op("tensor", lambda eng: eng.matmul(ps[6][:, 0:n], msf, foldv[0:16, 0:n],
                                               start=True, stop=True),
              reads=("fold", "msel_f"), writes=("ps6",), tag=f"{tg}:rbc")
        for srcp, dstb, sn, dn in ((pnA, pbA, "pnA", "pbA"), (pnB, pbB, "pnB", "pbB")):
            em.op("vector", lambda eng, s=srcp, d=dstb: eng.tensor_mul(
                d[:, 0:n], s[:, 0:n], ps[6][:, 0:n]),
                  reads=(sn, "ps6"), writes=(dn,), tag=f"{tg}:pm{sn}")
        if ti == 0:
            cx.tap("pbA", pbA[:, 0:NTILE], ("pbA",))
            cx.tap("pbB", pbB[:, 0:NTILE], ("pbB",))
        for s in range(nsub):
            for tb, srcb, sn in ((0, pbA, "pbA"), (1, pbB, "pbB")):
                lhsT = srcb[:, 128 * s:128 * (s + 1)]
                em.op("tensor", lambda eng, l_=lhsT, tb=tb: eng.matmul(
                    ps[4][:, 128 * tb:128 * (tb + 1)], l_, v("permT"), start=True, stop=True),
                      reads=(sn, "permT"), writes=("ps4",), tag=f"{tg}:pt{s}.{tb}")
            pnt3 = pntv.rearrange("p (h g) -> p h g", g=16)
            em.op("scalar", lambda eng: eng.copy(
                pnt3[:, :, 0:8], ps[4][:, 0:128].rearrange("p (h g) -> p h g", g=8)),
                  reads=("ps4",), writes=("pnt",), tag=f"{tg}:peA{s}")
            em.op("scalar", lambda eng: eng.copy(
                pnt3[:, :, 8:16], ps[4][:, 128:256].rearrange("p (h g) -> p h g", g=8)),
                  reads=("ps4",), writes=("pnt",), tag=f"{tg}:peB{s}")
            for h in range(NH):
                pin = pntv[:, 16 * h:16 * h + 16].unsqueeze(1).broadcast_to([128, DH, 16])
                vin = vdm[:, 1024 * s:1024 * (s + 1)].rearrange("p (d g) -> p d g", d=DH)
                peng = "vector" if h % 2 == 0 else "gpsimd"
                em.op(peng, lambda eng, pin=pin, vin=vin: eng.tensor_mul(
                    pvpv.rearrange("p (d g) -> p d g", d=DH), pin, vin),
                      reads=("pnt", "vdm"), writes=("pvp",), tag=f"{tg}:pp{s}.{h}")
                oslice = otmv[:, s * 1024 + 64 * h: s * 1024 + 64 * h + 64]
                def _pr(eng, o=oslice):
                    with cx.nc.allow_low_precision(reason="bf16 matmul operand"):
                        return eng.reduce_sum(o, pvpv.rearrange("p (d g) -> p d g", d=DH),
                                              axis=AX)
                em.op("vector", _pr, reads=("pvp",), writes=("otm",), tag=f"{tg}:pr{s}.{h}")
            for c in range(NCH):
                lhsT = otmv[:, s * 1024 + 128 * c: s * 1024 + 128 * (c + 1)]
                dst = obv[:, c * NTILE + 128 * s: c * NTILE + 128 * s + 128]
                em.op("tensor", lambda eng, l_=lhsT: eng.matmul(
                    ps[5][:, 0:128], l_, idn, start=True, stop=True),
                      reads=("otm", "ident"), writes=("ps5",), tag=f"{tg}:ot{s}.{c}")
                em.op("scalar", lambda eng, d=dst: eng.copy(d, ps[5][:, 0:128]),
                      reads=("ps5",), writes=("ob",), tag=f"{tg}:oe{s}.{c}")

        if ti == 0:
            cx.tap("otm", otmv[:, 0:4 * 1024], ("otm",))
            cx.tap("obm", obv[:, 0:NCH * NTILE], ("ob",))
        for mc in range(NCH):
            bank = mc % 2
            for kc in range(NCH):
                lhsT = wot[:, kc * DM + 128 * mc: kc * DM + 128 * mc + 128]
                rhs = obv[:, kc * NTILE:kc * NTILE + n]
                em.op("tensor", lambda eng, l_=lhsT, r_=rhs, f=(kc == 0),
                      s=(kc == NCH - 1), bank=bank: eng.matmul(
                          ps[bank][:, 0:n], l_, r_, start=f, stop=s),
                      reads=("wout", "ob"), writes=(f"ps{bank}",), tag=f"{tg}:wo{mc}.{kc}")
            em.op("scalar", lambda eng, mc=mc, bank=bank: eng.add(
                ps[bank][:, 0:n], ps[bank][:, 0:n], boc[:, mc:mc + 1]),
                  reads=(f"ps{bank}", "bout_c"), writes=(f"ps{bank}",), tag=f"{tg}:bo{mc}")
            xa = xtv[:, mc * XSTR + 1: mc * XSTR + 1 + n]
            em.op("vector", lambda eng, xa=xa, bank=bank: eng.tensor_add(
                xa, xa, ps[bank][:, 0:n]),
                  reads=(f"ps{bank}", "xt"), writes=("xt",), tag=f"{tg}:res{mc}")
        if ti == 0:
            cx.tap("xres", xtv[:, 0:NCH * XSTR], ("xt",))
        xt3 = xtv.rearrange("p (c j) -> p c j", c=NCH)
        xb3 = xbv.rearrange("p (c j) -> p c j", c=NCH)
        em.op("vector", lambda eng: eng.tensor_copy(xb3[:, :, 1:n + 1], xt3[:, :, 1:n + 1]),
              reads=("xt",), writes=("xb",), tag=f"{tg}:reb")
        if ti == 0:
            cx.tap("xbreb", xbv[:, 0:NCH * XSTR], ("xb",))
        emit_ln(cx, n,
                sq_ap_fn=lambda c: xbv[:, c * XSTR + 1: c * XSTR + 1 + n],
                mu_ap_fn=lambda c: xbv[:, c * XSTR + 1: c * XSTR + 1 + n],
                src_reads=("xb",),
                xin_ap_fn=lambda c: xtv[:, c * XSTR + 1: c * XSTR + 1 + n],
                xin_reads=("xt",),
                out_ap_fn=lambda c: xtv[:, c * XSTR + 1: c * XSTR + 1 + n],
                out_writes=("xt",), tag=tg + ":oln")
        store_xtile(cx, "x1", t0, n, tg)

    t0 = 0
    for ti, n in enumerate(TILES):
        _tile(ti, n, t0)
        t0 += n


def emit_final(cx):
    em, ps = cx.em, cx.ps
    em.barrier("fin")
    _phase_layout(cx.views, cx.pers_end, "fin")
    v = cx.views
    neg_ones_row(cx, "fin")
    ld_direct(cx, "onz", "gbrow", 2, DM)
    stream_in = "B"
    x1v = v("x1t")
    xtv, xbv = v("xt"), v("xb")
    x1sr = cx.streams["x1"].rearrange("(c p) t -> p c t", p=128)
    def _tile(ti, n, t0):
        tg = f"z.{ti}"
        ld_xtile(cx, stream_in, t0, n, tg, halo=False)
        em.op("sync", lambda eng, t0=t0: eng.dma_start(
            out=x1v.rearrange("p (c j) -> p c j", c=NCH)[:, :, 0:n],
            in_=x1sr[:, :, t0:t0 + n]),
              writes=("x1t",), tag=f"{tg}:lx1")
        for c in range(NCH):
            xa = xtv[:, c * XSTR + 1: c * XSTR + 1 + n]
            x1a = x1v[:, c * NTILE:c * NTILE + n]
            em.op("vector", lambda eng, xa=xa, x1a=x1a: eng.tensor_add(xa, xa, x1a),
                  reads=("x1t", "xt"), writes=("xt",), tag=f"{tg}:ad{c}")
        xt3 = xtv.rearrange("p (c j) -> p c j", c=NCH)
        xb3 = xbv.rearrange("p (c j) -> p c j", c=NCH)
        em.op("vector", lambda eng: eng.tensor_copy(xb3[:, :, 1:n + 1], xt3[:, :, 1:n + 1]),
              reads=("xt",), writes=("xb",), tag=f"{tg}:reb")
        emit_ln(cx, n,
                sq_ap_fn=lambda c: xbv[:, c * XSTR + 1: c * XSTR + 1 + n],
                mu_ap_fn=lambda c: xbv[:, c * XSTR + 1: c * XSTR + 1 + n],
                src_reads=("xb",),
                xin_ap_fn=lambda c: xtv[:, c * XSTR + 1: c * XSTR + 1 + n],
                xin_reads=("xt",),
                out_ap_fn=lambda c: xtv[:, c * XSTR + 1: c * XSTR + 1 + n],
                out_writes=("xt",), tag=tg + ":fln")
        store_xtile(cx, "out", t0, n, tg)

    t0 = 0
    for ti, n in enumerate(TILES):
        _tile(ti, n, t0)
        t0 += n


def build_program():
    nc = bass.Bass("TRN2", target_bir_lowering=False, detect_race_conditions=False)
    em = Em()
    D = {}
    def din(name, shape, dtype=F32):
        D[name] = nc.dram_tensor(name, list(shape), dtype, kind="ExternalInput").ap()
    din("x_fm", (DM, T_CORE))
    for nm in ("C0", "S0", "Cq", "Sq", "Ck", "Sk"):
        din(nm, (128, T_CORE))
    din("masks", (3, 128, KSPAN))
    for nm, sh in [("Wq", (DM, DM)), ("bq", (128, 8)), ("Wdown", (DM, DLAT)),
                   ("bdown", (128, 2)), ("Wup_k", (DLAT, DM)), ("Wup_v", (DLAT, DM)),
                   ("bup_k", (128, 8)), ("bup_v_row", (1, DM)), ("Wout", (DM, DM)),
                   ("bout", (128, 8)), ("identM", (128, 128)), ("swapP", (128, 128)),
                   ("gramE", (128, 128)), ("gramR", (128, 256)), ("msel", (16, 128)),
                   ("hsel", (16, 8 * 128)), ("permT", (128, 128)),
                   ("ones_row", (1, 128)), ("zeros_col", (128, 1)), ("ones_col", (128, 1)),
                   ("onz", (2, DM)), ("negrow", (1, NTILE))]:
        din(nm, sh)
    for l in range(DEPTH):
        din(f"Wqkv{l}", (DM, 3 * DM)); din(f"Wo{l}", (DM, DM))
        din(f"W1a{l}", (DM, DI_PAD)); din(f"W1g{l}", (DM, DI_PAD)); din(f"W2{l}", (DI_PAD, DM))
        din(f"b1a{l}", (128, KCH)); din(f"b1g{l}", (128, KCH))
        din(f"b2{l}", (128, 8)); din(f"lngb{l}", (2, DM))
        din(f"qs{l}", (128, 1)); din(f"ks{l}", (128, 1))
    out = nc.dram_tensor("out_fm", [DM, T_CORE], F32, kind="ExternalOutput").ap()
    _k = "ExternalOutput" if DEBUG_STREAMS else "Internal"
    sA = nc.dram_tensor("sA", [DM, T_CORE], F32, kind=_k).ap()
    sB = nc.dram_tensor("sB", [DM, T_CORE], F32, kind=_k).ap()
    x1s = nc.dram_tensor("x1s", [DM, T_CORE], F32, kind=_k).ap()
    streams = {"x": D["x_fm"], "A": sA, "B": sB, "x1": x1s, "out": out}

    with contextlib.ExitStack() as es:
        E = es.enter_context
        sems = {e: E(nc.semaphore(f"sem_{e}")) for e in Em.ENGINES}
        for i in range(Em.NDMA_SEMS):
            sems[f"dma{i}"] = E(nc.semaphore(f"dsem{i}"))
        base = E(nc.sbuf_tensor("POOL", [128, SBYTES // 2], BF16))
        views = Views(base)
        pers_end = _static_layout(views)
        ps = [E(nc.psum_tensor(f"ps{i}", [128, 512], F32)) for i in range(8)]
        block = E(nc.Block())
        cx = Ctx(nc, em, D, views, ps, streams, pers_end)
        emit_consts(cx)
        for ph in PIPELINE:
            if ph == "mla":
                emit_mla(cx)
            elif ph.startswith("att"):
                emit_attn(cx, int(ph[3:]))
            elif ph.startswith("ffn"):
                emit_ffn(cx, int(ph[3:]))
            elif ph == "fin":
                emit_final(cx)
        em.op("gpsimd", lambda eng: None, extra_deps=cx.store_ops[-8:], tag="final_wait",
              wait_only=True)
        em.finalize_and_emit(block, sems)
    return nc


def kernel(**inputs):
    W = _host_weights(inputs)
    x = np.asarray(inputs["L0_attn_x"], dtype=np.float32)
    in_maps = []
    for b in range(B):
        for half in range(2):
            start = 0 if half == 0 else START1
            im = dict(W)
            im["x_fm"] = np.ascontiguousarray(x[b, start:start + T_CORE].T)
            im.update(_rope_tables(np.arange(start, start + T_CORE)))
            im["masks"] = _masks(first=(half == 0))
            in_maps.append(im)
    nc = build_program()
    res = run_bass_kernel_spmd(nc, in_maps, core_ids=list(range(8)))
    if DEBUG_STREAMS:
        kernel._dbg = res.results
    outp = np.empty((B, T, DM), np.float32)
    for b in range(B):
        outp[b, :T_CORE] = res.results[2 * b]["out_fm"].T
        outp[b, T_CORE:] = res.results[2 * b + 1]["out_fm"].T[T_CORE - (T - T_CORE):]
    return outp
